# version 05
# baseline (speedup 1.0000x reference)
import math

import numpy as np

import concourse.bass as bass
import concourse.mybir as mybir
from concourse.bass_utils import run_bass_kernel_spmd
import concourse.library_config as libcfg

F32 = mybir.dt.float32
BF16 = mybir.dt.bfloat16
AF = mybir.ActivationFunctionType
ALU = mybir.AluOpType
AX = mybir.AxisListType

T = 512
NEG = -1.0e30
PI = math.pi
LN1E4 = math.log(10000.0)

ENGS = ("pe", "dve", "act", "pool", "sp")
_BLOCKSECT = {"pe": "tensor", "dve": "vector", "act": "scalar",
              "pool": "gpsimd", "sp": "sync"}

PROXY_LIB = [l for l in libcfg.all_libraries if l.name == "proxy"][0]


class Chan:
    __slots__ = ("name", "sem", "issued")

    def __init__(self, name, sem):
        self.name = name
        self.sem = sem
        self.issued = 0


class Bld:

    def __init__(self, nc):
        self.nc = nc
        self.ops = {e: [] for e in ENGS}
        self.sem = {e: nc.alloc_semaphore("tk_" + e) for e in ENGS}
        self.n = {e: 0 for e in ENGS}
        self.wm = {e: {} for e in ENGS}
        self.zw = {}
        self.zr = {}
        self.chans = []

    def chan(self, name):
        c = Chan(name, self.nc.alloc_semaphore("ch_" + name))
        self.chans.append(c)
        return c

    def _wait(self, e, key, val):
        if isinstance(key, Chan):
            val = 16 * key.issued
        if val <= 0:
            return
        if self.wm[e].get(key, 0) >= val:
            return
        self.wm[e][key] = val
        semh = key.sem if isinstance(key, Chan) else self.sem[key]
        self.ops[e].append(("w", semh, val))

    def _dep(self, e, zones, war):
        for z in zones:
            lw = self.zw.get(z)
            if lw is not None:
                self._wait(e, lw[0], lw[1])
            if war:
                for k, v in self.zr.get(z, {}).items():
                    self._wait(e, k, v)

    def emit(self, e, fn, reads=(), writes=()):
        self._dep(e, reads, False)
        self._dep(e, writes, True)
        self.ops[e].append(("i", fn, self.sem[e]))
        self.n[e] += 1
        t = self.n[e]
        for z in reads:
            self.zr.setdefault(z, {})[e] = t
        for z in writes:
            self.zw[z] = (e, t)
            self.zr[z] = {}

    def dma(self, ch, out_ap, in_ap, reads=(), writes=(), eng="sp"):
        self._dep(eng, reads, False)
        self._dep(eng, writes, True)
        self.ops[eng].append(("d", out_ap, in_ap, ch.sem))
        ch.issued += 1
        for z in reads:
            self.zr.setdefault(z, {})[ch] = 0
        for z in writes:
            self.zw[z] = (ch, 0)
            self.zr[z] = {}

    def barrier(self):
        for e1 in ENGS:
            for e2 in ENGS:
                if e1 != e2:
                    self._wait(e1, e2, self.n[e2])

    def wait_all_chans(self, e="sp"):
        for c in self.chans:
            self._wait(e, c, 1)

    def replay(self):
        with self.nc.Block() as block:
            for e in ENGS:
                ops = self.ops[e]

                def body(eng, ops=ops):
                    for op in ops:
                        if op[0] == "w":
                            eng.wait_ge(op[1], op[2])
                        elif op[0] == "i":
                            r = op[1](eng)
                            if isinstance(r, list):
                                r = r[-1]
                            r.then_inc(op[2], 1)
                        else:
                            eng.dma_start(op[1], op[2]).then_inc(op[3], 16)
                getattr(block, _BLOCKSECT[e])(body)


def _mmg(parts):
    def fn(eng):
        last = None
        for (o, l, r, s, st, tp) in parts:
            if tp is None:
                last = eng.matmul(o, l, r, start=s, stop=st)
            else:
                last = eng.matmul(o, l, r, start=s, stop=st, tile_position=tp)
        return last
    return fn


def build_core_program(NT=4096):
    NC = NT // T
    nc = bass.Bass("TRN2", target_bir_lowering=False)

    xT = nc.dram_tensor("xT", [1024, NT], F32, kind="ExternalInput").ap()
    wqkvA = nc.dram_tensor("wqkvA", [24, 128 * 8 * 128], F32, kind="ExternalInput").ap()
    woutA = nc.dram_tensor("woutA", [8, 128 * 8 * 128], F32, kind="ExternalInput").ap()
    wqkvF = [nc.dram_tensor(f"wqkvF{l}", [12, 128 * 8 * 128], F32, kind="ExternalInput").ap() for l in range(2)]
    woF = [nc.dram_tensor(f"woF{l}", [8, 128 * 4 * 128], F32, kind="ExternalInput").ap() for l in range(2)]
    wfus = [nc.dram_tensor(f"wfus{l}", [22, 128 * 24 * 128], F32, kind="ExternalInput").ap() for l in range(2)]
    b1a_d = [nc.dram_tensor(f"b1a{l}", [128, 22], F32, kind="ExternalInput").ap() for l in range(2)]
    b1g_d = [nc.dram_tensor(f"b1g{l}", [128, 22], F32, kind="ExternalInput").ap() for l in range(2)]
    b2_d = [nc.dram_tensor(f"b2{l}", [128, 8], F32, kind="ExternalInput").ap() for l in range(2)]
    lng_d = [nc.dram_tensor(f"lng{l}", [128, 8], F32, kind="ExternalInput").ap() for l in range(2)]
    lnb_d = [nc.dram_tensor(f"lnb{l}", [128, 8], F32, kind="ExternalInput").ap() for l in range(2)]
    qsc_d = [nc.dram_tensor(f"qsc{l}", [128, 2], F32, kind="ExternalInput").ap() for l in range(2)]
    ksc_d = [nc.dram_tensor(f"ksc{l}", [128, 2], F32, kind="ExternalInput").ap() for l in range(2)]
    outT = nc.dram_tensor("outT", [1024, NT], F32, kind="ExternalOutput").ap()
    acc_s = nc.dram_tensor("acc_s", [1024, NT], F32, kind="Internal").ap()
    y1_s = nc.dram_tensor("y1_s", [1024, NT], F32, kind="Internal").ap()
    rott = [nc.dram_tensor(f"rott{l}", [4 * 128, NT], F32, kind="Internal").ap() for l in range(2)]

    A = nc.alloc_sbuf_tensor
    XN = A("XN", [128, 8 * 513], F32)
    X = A("Xt", [128, 8 * 512], F32)
    BIGQ = A("BIGQ", [128, 6152], F32)
    X1OFF = 2048
    ACC = A("ACCt", [128, 8 * 512], F32)
    O = A("Ot", [128, 8 * 512], F32)
    KR = A("KRt", [128, 6144], F32)
    VT = A("VTt", [128, 6144], F32)
    WS = [A("WS0", [128, 24 * 128], F32), A("WS1", [128, 24 * 128], F32)]
    P = A("Pt", [128, 1024], F32)
    PTS = A("PTSt", [128, 1024], F32)
    Mt = A("Mt", [128, 256], F32)
    MG = A("MGt", [128, 256], F32)
    QR = A("QRt", [128, 512], F32)
    VS = A("VSt", [128, 512], F32)
    SQA = A("SQAt", [128, 512], F32)
    SQB = A("SQBt", [128, 512], F32)
    MB = A("MBt", [128, 512], F32)
    RB = A("RBt", [128, 512], F32)
    RT = A("RTt", [128, 2048], F32)
    MA = A("MAt", [128, 1024], BF16)
    MF = A("MFt", [128, 4096], BF16)
    SM = A("SMt", [128, 96], F32)
    NMT = A("NMTt", [128, 8], F32)
    SMD = A("SMDt", [1, 256], F32)
    RBC = A("RBCt", [128, 128], F32)
    IDENT = A("IDENTt", [128, 128], F32)
    ONES = A("ONESt", [128, 1], F32)
    ONES2 = A("ONES2t", [128, 2], F32)
    GXN = A("GXNt", [128, 8], F32)
    GX1 = A("GX1t", [128, 8], F32)
    RSS = A("RSSt", [1, 512], F32)
    RROT = A("RROTt", [128, 128], F32)
    BONE = A("BONEt", [1, 128], F32)
    CEPS = A("CEPSt", [128, 1], F32)
    IOD = A("IODt", [128, 4], F32)
    B1A = [A(f"B1A{l}", [128, 22], F32) for l in range(2)]
    B1G = [A(f"B1G{l}", [128, 22], F32) for l in range(2)]
    B2 = [A(f"B2{l}", [128, 8], F32) for l in range(2)]
    LNG = [A(f"LNG{l}", [128, 8], F32) for l in range(2)]
    LNB = [A(f"LNB{l}", [128, 8], F32) for l in range(2)]
    QSC = [A(f"QSC{l}", [128, 2], F32) for l in range(2)]
    KSC = [A(f"KSC{l}", [128, 2], F32) for l in range(2)]

    def Qa(a, b_, p0=0, p1=128):
        return BIGQ[p0:p1, a:b_]

    def X1s(a, b_):
        return BIGQ[:, X1OFF + a:X1OFF + b_]

    PS = [nc.alloc_psum_tensor(f"PS{i}", [128, 512], F32) for i in range(8)]

    b = Bld(nc)
    CX = b.chan("cx")
    CW = [b.chan("cw0"), b.chan("cw1")]
    CRT = b.chan("crt")
    CB = b.chan("cb")
    CO = b.chan("co")
    CTW = b.chan("ctw")

    psz = [f"ps{i}" for i in range(8)]

    b.emit("dve", lambda e: e.memset(BONE[:, :], 1.0), writes=("BONE",))

    b.emit("dve", lambda e: e.memset(IDENT[:, :], 1.0), writes=("IDENT",))
    b.emit("pool", lambda e: e.affine_select(IDENT[:, :], IDENT[:, :], [[1, 128]],
                                             ALU.is_equal, 0.0, base=0,
                                             channel_multiplier=-1),
           reads=("lib",), writes=("IDENT",))
    b.emit("dve", lambda e: e.memset(ONES[:, :], 1.0), writes=("ONES",))
    b.emit("dve", lambda e: e.memset(CEPS[:, :], 1e-5), writes=("CEPS",))
    b.emit("dve", lambda e: e.memset(ONES2[:, :], 1.0), writes=("ONES2",))
    b.emit("pool", lambda e: e.affine_select(ONES2[:, 0:1], ONES2[:, 0:1], [[1, 1]],
                                             ALU.is_ge, 0.0, base=63,
                                             channel_multiplier=-1),
           reads=("lib",), writes=("ONES2",))
    b.emit("pool", lambda e: e.affine_select(ONES2[:, 1:2], ONES2[:, 1:2], [[1, 1]],
                                             ALU.is_ge, 0.0, base=-64,
                                             channel_multiplier=1),
           reads=("lib",), writes=("ONES2",))
    for (c0, sgn, kb) in ((0, -1.0, -32), (32, 1.0, 0), (64, -1.0, -96), (96, 1.0, -64)):
        sl = RROT[:, c0:c0 + 32]
        b.emit("dve", lambda e, sl=sl, sgn=sgn: e.memset(sl, sgn), writes=("RROT",))
        b.emit("pool", lambda e, sl=sl, kb=kb: e.affine_select(
            sl, sl, [[-1, 32]], ALU.is_equal, 0.0, base=kb,
            channel_multiplier=1), reads=("lib",), writes=("RROT",))
    b.emit("dve", lambda e: e.memset(MA[:, :], 0.0), writes=("MA",))
    b.emit("dve", lambda e: e.memset(MF[:, :], 0.0), writes=("MF",))
    for qs in range(2):
        sl = MA[:, qs * 512:(qs + 1) * 512]
        b.emit("pool", lambda e, sl=sl, qs=qs: e.affine_select(
            sl, sl, [[1, 512]], ALU.is_ge, NEG, base=-(128 * qs) - 1,
            channel_multiplier=-1), reads=("lib",), writes=("MA",))
        b.emit("pool", lambda e, sl=sl, qs=qs: e.affine_select(
            sl, sl, [[-1, 512]], ALU.is_ge, NEG, base=(128 * qs) + 256,
            channel_multiplier=1), reads=("lib",), writes=("MA",))
    for qs in range(4):
        sl = MF[:, qs * 1024:(qs + 1) * 1024]
        b.emit("pool", lambda e, sl=sl, qs=qs: e.affine_select(
            sl, sl, [[1, 1024]], ALU.is_ge, NEG, base=-(128 * qs),
            channel_multiplier=-1), reads=("lib",), writes=("MF",))
        b.emit("pool", lambda e, sl=sl, qs=qs: e.affine_select(
            sl, sl, [[-1, 1024]], ALU.is_ge, NEG, base=(128 * qs) + 512,
            channel_multiplier=1), reads=("lib",), writes=("MF",))
    for l in range(2):
        b.dma(CB, B1A[l][:, :], b1a_d[l], writes=("BI",))
        b.dma(CB, B1G[l][:, :], b1g_d[l], writes=("BI",))
        b.dma(CB, B2[l][:, :], b2_d[l], writes=("BI",))
        b.dma(CB, LNG[l][:, :], lng_d[l], writes=("BI",))
        b.dma(CB, LNB[l][:, :], lnb_d[l], writes=("BI",))
        b.dma(CB, QSC[l][:, :], qsc_d[l], writes=("BI",))
        b.dma(CB, KSC[l][:, :], ksc_d[l], writes=("BI",))

    for qtr in range(4):
        b.emit("pool", lambda e, q=qtr: e.iota(
            IOD[q * 32:(q + 1) * 32, 0:1], [[1, 1]], base=0, channel_multiplier=1,
            allow_small_or_imprecise_dtypes=True), reads=("lib",), writes=("IOD",))
    b.emit("act", lambda e: e.activation(IOD[:, 1:2], IOD[:, 0:1], AF.Exp,
                                         scale=-2.0 * LN1E4 / 64.0),
           reads=("IOD",), writes=("IOD",))
    b.emit("dve", lambda e: e.tensor_scalar(IOD[:, 2:3], IOD[:, 0:1],
                                            2.0 / 89.6, 25.6 / 89.6,
                                            ALU.mult, ALU.add),
           reads=("IOD",), writes=("IOD",))
    b.emit("act", lambda e: e.activation(IOD[:, 3:4], IOD[:, 2:3], AF.Ln),
           reads=("IOD",), writes=("IOD",))
    for l in range(2):
        for cb in range(NC):
            t0 = cb * 512
            b.emit("pool", lambda e, t0=t0: e.iota(
                MB[:, :], [[1, 512]], base=t0, channel_multiplier=0,
                allow_small_or_imprecise_dtypes=True),
                reads=("lib",), writes=("MB",))
            b.emit("dve", lambda e: e.tensor_scalar_mul(RB[:, :], MB[:, :], IOD[:, 1:2]),
                   reads=("MB", "IOD"), writes=("RB",))
            b.emit("dve", lambda e: e.tensor_scalar_add(P[:, 0:512], RB[:, :], PI),
                   reads=("RB",), writes=("P",))
            for j in range(9, -1, -1):
                cj = 2.0 * PI * (1 << j)
                b.emit("dve", lambda e, cj=cj: e.tensor_single_scalar(
                    VS[:, :], P[:, 0:512], cj, ALU.is_ge),
                    reads=("P",), writes=("VS",))
                b.emit("dve", lambda e, cj=cj: e.scalar_tensor_tensor(
                    P[:, 0:512], VS[:, :], -cj, P[:, 0:512], ALU.mult, ALU.add),
                    reads=("VS",), writes=("P",))
            b.emit("dve", lambda e: e.tensor_scalar_add(P[:, 0:512], P[:, 0:512], -PI),
                   reads=(), writes=("P",))
            b.emit("dve", lambda e: e.tensor_scalar_add(P[:, 512:1024], P[:, 0:512], PI / 2),
                   reads=(), writes=("P",))
            b.emit("dve", lambda e: e.tensor_single_scalar(
                VS[:, :], P[:, 512:1024], PI, ALU.is_ge),
                reads=("P",), writes=("VS",))
            b.emit("dve", lambda e: e.scalar_tensor_tensor(
                P[:, 512:1024], VS[:, :], -2.0 * PI, P[:, 512:1024],
                ALU.mult, ALU.add),
                reads=("VS",), writes=("P",))
            b.emit("act", lambda e: e.activation(SQB[:, :], P[:, 0:512], AF.Sin),
                   reads=("P",), writes=("SQB",))
            b.emit("act", lambda e: e.activation(VS[:, :], P[:, 512:1024], AF.Sin),
                   reads=("P",), writes=("VS",))
            b.emit("pool", lambda e, t0=t0: e.iota(
                QR[:, :], [[1, 512]], base=t0 - 2048, channel_multiplier=0,
                allow_small_or_imprecise_dtypes=True),
                reads=("lib",), writes=("QR",))
            b.emit("dve", lambda e: e.tensor_scalar_mul(QR[:, :], QR[:, :], IOD[:, 3:4]),
                   reads=("IOD",), writes=("QR",))
            b.emit("act", lambda e: e.activation(MB[:, :], QR[:, :], AF.Exp,
                                                 scale=1.0 / 256.0),
                   reads=("QR",), writes=("MB",))
            b.emit("act", lambda e: e.activation(RB[:, :], QR[:, :], AF.Exp,
                                                 scale=-1.0 / 256.0),
                   reads=("QR",), writes=("RB",))
            tabs = [(0, VS, MB, QSC[l][:, 0:1]),
                    (512, SQB, MB, QSC[l][:, 1:2]),
                    (1024, VS, RB, KSC[l][:, 0:1]),
                    (1536, SQB, RB, KSC[l][:, 1:2])]
            for (off, t1, t2, sc) in tabs:
                b.emit("dve", lambda e, off=off, t1=t1, t2=t2: e.tensor_mul(
                    RT[:, off:off + 512], t1[:, :], t2[:, :]),
                    reads=("VS", "SQB", "MB", "RB"), writes=("RT",))
                b.emit("dve", lambda e, off=off, sc=sc: e.tensor_scalar_mul(
                    RT[:, off:off + 512], RT[:, off:off + 512], sc),
                    reads=("BI",), writes=("RT",))
            for i in range(4):
                b.dma(CTW, rott[l][i * 128:(i + 1) * 128, t0:t0 + 512],
                      RT[:, i * 512:(i + 1) * 512],
                      reads=("RT",), writes=(f"d:rott{l}",))

    b.barrier()


    def emit_ln_stats_and_apply(src_tile, src_cols, dst_writes, dst_slices,
                                affine=None):
        for kk in range(8):
            sq = SQA if kk % 2 == 0 else SQB
            sqz = "SQA" if kk % 2 == 0 else "SQB"
            src = src_cols(kk)
            b.emit("act", lambda e, sq=sq, src=src: e.activation(sq[:, :], src, AF.Square),
                   reads=(src_tile,), writes=(sqz,))
            b.emit("pe", _mmg([(PS[6][0:1, 0:512], ONES[:, 0:1], src,
                                kk == 0, kk == 7, None)]),
                   reads=(src_tile, "ONES"), writes=("ps6",))
            b.emit("pe", _mmg([(PS[7][0:1, 0:512], ONES[:, 0:1], sq[:, :],
                                kk == 0, kk == 7, None)]),
                   reads=(sqz, "ONES"), writes=("ps7",))
        b.emit("dve", lambda e: e.tensor_scalar_mul(MB[0:1, :], PS[6][0:1, 0:512], 1.0 / 1024.0),
               reads=("ps6",), writes=("MB",))
        b.emit("dve", lambda e: e.tensor_scalar_mul(RB[0:1, :], PS[7][0:1, 0:512], 1.0 / 1024.0),
               reads=("ps7",), writes=("RB",))
        b.emit("act", lambda e: e.activation(RSS[0:1, :], MB[0:1, :], AF.Square),
               reads=("MB",), writes=("RSS",))
        b.emit("dve", lambda e: e.tensor_sub(RB[0:1, :], RB[0:1, :], RSS[0:1, :]),
               reads=("RSS",), writes=("RB",))
        b.emit("act", lambda e: e.activation(RSS[0:1, :], RB[0:1, :], AF.Ln,
                                             bias=CEPS[0:1, 0:1]),
               reads=("RB", "CEPS"), writes=("RSS",))
        b.emit("act", lambda e: e.activation(RB[0:1, :], RSS[0:1, :], AF.Exp, scale=-0.5),
               reads=("RSS",), writes=("RB",))
        b.emit("pe", _mmg([(PS[6][:, 0:512], BONE[0:1, 0:128], MB[0:1, :],
                            True, True, None)]),
               reads=("MB", "BONE"), writes=("ps6",))
        b.emit("pe", _mmg([(PS[7][:, 0:512], BONE[0:1, 0:128], RB[0:1, :],
                            True, True, None)]),
               reads=("RB", "BONE"), writes=("ps7",))
        for kk in range(8):
            src = src_cols(kk)
            dst = dst_slices(kk)
            b.emit("dve", lambda e, d=dst, s=src: e.tensor_sub(d, s, PS[6][:, 0:512]),
                   reads=(src_tile, "ps6"), writes=dst_writes)
            b.emit("dve", lambda e, d=dst: e.tensor_mul(d, d, PS[7][:, 0:512]),
                   reads=("ps7",), writes=dst_writes)
            if affine is not None:
                g_ap, bt_ap = affine
                b.emit("dve", lambda e, d=dst, g=g_ap(kk), bb=bt_ap(kk): e.tensor_scalar(
                    d, d, g, bb, ALU.mult, ALU.add),
                    reads=("BI",), writes=dst_writes)

    def wslot_load(widx, dram_ap, j, nblk):
        s = widx % 2
        b.dma(CW[s], WS[s][:, 0:nblk * 128], dram_ap[j:j + 1, :],
              writes=(f"WS{s}",))
        return s

    for c in range(NC):
        ccol = c * 512
        for kk in range(8):
            b.dma(CX, X[:, kk * 512:(kk + 1) * 512],
                  xT[kk * 128:(kk + 1) * 128, ccol:ccol + 512], writes=("X",))
        emit_ln_stats_and_apply(
            "X", lambda kk: X[:, kk * 512:(kk + 1) * 512], ("XN",),
            dst_slices=lambda kk: XN[:, kk * 513 + 1:kk * 513 + 513])

        s0 = (2 * c) % 3
        s1 = (2 * c + 1) % 3
        wslot_load(0, wqkvA, 0, 8)
        for m in range(24):
            s = m % 2
            if m + 1 < 24:
                wslot_load(m + 1, wqkvA, m + 1, 8)
            parts = [(PS[s][:, 0:512], WS[s][:, kk * 128:(kk + 1) * 128],
                      XN[:, kk * 513 + 1:kk * 513 + 513], kk == 0, kk == 7, None)
                     for kk in range(8)]
            b.emit("pe", _mmg(parts), reads=(f"WS{s}", "XN"), writes=(psz[s],))
            if m < 8:
                b.emit("act", lambda e, m=m, s=s: e.copy(Qa(m * 512, (m + 1) * 512),
                                                         PS[s][:, 0:512]),
                       reads=(psz[s],), writes=("Q",))
            elif m < 16:
                j = m - 8
                d0 = KR[:, j * 768 + s0 * 256:j * 768 + s0 * 256 + 256]
                d1 = KR[:, j * 768 + s1 * 256:j * 768 + s1 * 256 + 256]
                b.emit("act", lambda e, d=d0, s=s: e.copy(d, PS[s][:, 0:256]),
                       reads=(psz[s],), writes=(f"KR{s0}",))
                b.emit("act", lambda e, d=d1, s=s: e.copy(d, PS[s][:, 256:512]),
                       reads=(psz[s],), writes=(f"KR{s1}",))
            else:
                j = m - 16
                b.emit("act", lambda e, s=s: e.copy(VS[:, :], PS[s][:, 0:512]),
                       reads=(psz[s],), writes=("VS",))
                for u in range(4):
                    b.emit("pe", lambda e, u=u: e.transpose(
                        PS[2][:, u * 128:(u + 1) * 128],
                        VS[:, u * 128:(u + 1) * 128], IDENT[:, :]),
                        reads=("VS", "IDENT"), writes=("ps2",))
                for half in range(2):
                    sl = s0 if half == 0 else s1
                    src = PS[2][:, half * 256:half * 256 + 256]
                    dst = bass.AP(VT, sl * 2048 + j * 128,
                                  [[6144, 128], [1024, 2], [1, 128]])
                    b.emit("act", lambda e, d=dst, sp=src: e.copy(d, sp),
                           reads=("ps2",), writes=(f"VT{sl}",))

        for b_loc in range(2):
            blk = 2 * c + b_loc
            spv = (blk - 1) % 3
            scv = blk % 3
            first = (blk == 0)
            o_off = 256 if first else 0
            for hp in range(8):
                for qs in range(2):
                    qcols = hp * 512 + b_loc * 256 + qs * 128
                    cidx = ((b_loc * 8 + hp) * 2 + qs) * 2
                    for h in range(2):
                        sbk = 3 + h
                        parts = []
                        if not first:
                            parts.append((PS[sbk][:, 0:256],
                                          Qa(qcols, qcols + 128, 64 * h, 64 * h + 64),
                                          KR[64 * h:64 * h + 64,
                                             hp * 768 + spv * 256:hp * 768 + spv * 256 + 256],
                                          True, False, None))
                        parts.append((PS[sbk][:, 256:512],
                                      Qa(qcols, qcols + 128, 64 * h, 64 * h + 64),
                                      KR[64 * h:64 * h + 64,
                                         hp * 768 + scv * 256:hp * 768 + scv * 256 + 256],
                                      first, True, None))
                        rd = ("Q", f"KR{scv}") if first else ("Q", f"KR{spv}", f"KR{scv}")
                        b.emit("pe", _mmg(parts), reads=rd, writes=(psz[sbk],))
                    for h in range(2):
                        sbk = 3 + h
                        b.emit("dve", lambda e, sbk=sbk, o=o_off, qs=qs: e.tensor_add(
                            PS[sbk][:, o:512], PS[sbk][:, o:512],
                            MA[:, qs * 512 + o:qs * 512 + 512]),
                            reads=("MA",), writes=(psz[sbk],))
                        b.emit("dve", lambda e, sbk=sbk, o=o_off, h=h: e.reduce_max(
                            NMT[:, h:h + 1], PS[sbk][:, o:512], axis=AX.X, negate=True),
                            reads=(psz[sbk],), writes=("NMT",))
                    b.emit("dve", lambda e: e.tensor_scalar_mul(NMT[:, 2:4], NMT[:, 0:2], 0.125),
                           reads=(), writes=("NMT",))
                    for h in range(2):
                        sbk = 3 + h
                        b.emit("act", lambda e, sbk=sbk, o=o_off, h=h, ci=cidx: e.activation(
                            P[:, h * 512 + o:h * 512 + 512], PS[sbk][:, o:512], AF.Exp,
                            bias=NMT[:, 2 + h:3 + h], scale=0.125,
                            accum_out=SM[:, ci + h:ci + h + 1]),
                            reads=(psz[sbk], "NMT"), writes=("P", "SM"))
                    b.emit("pe", lambda e, ci=cidx: [
                        e.transpose(PS[6][0:1, 0:128], SM[:, ci:ci + 1], IDENT[:, :]),
                        e.transpose(PS[6][0:1, 128:256], SM[:, ci + 1:ci + 2], IDENT[:, :])],
                        reads=("SM", "IDENT"), writes=("ps6",))
                    b.emit("dve", lambda e: e.reciprocal(SMD[0:1, :], PS[6][0:1, 0:256]),
                           reads=("ps6",), writes=("SMD",))
                    b.emit("pe", _mmg([(PS[6][0:64, 0:128], BONE[0:1, 0:64],
                                        SMD[0:1, 0:128], True, True, (0, 0)),
                                       (PS[6][64:128, 0:128], BONE[0:1, 0:64],
                                        SMD[0:1, 128:256], True, True, (0, 64))]),
                           reads=("SMD", "BONE"), writes=("ps6",))
                    b.emit("act", lambda e: e.copy(RBC[:, :], PS[6][:, 0:128]),
                           reads=("ps6",), writes=("RBC",))
                    jl = [2, 3] if first else [0, 1, 2, 3]
                    for h in range(2):
                        for j in jl:
                            b.emit("pe", lambda e, h=h, j=j: e.transpose(
                                PS[5][:, j * 128:(j + 1) * 128],
                                P[:, h * 512 + j * 128:h * 512 + (j + 1) * 128],
                                IDENT[:, :]),
                                reads=("P", "IDENT"), writes=("ps5",))
                        b.emit("act", lambda e, o=o_off: e.copy(
                            PTS[:, o:512], PS[5][:, o:512]),
                            reads=("ps5",), writes=("PTS",))
                        parts = []
                        vread = set()
                        for j in jl:
                            sl = spv if j < 2 else scv
                            ts = j % 2
                            vread.add(f"VT{sl}")
                            base = sl * 2048 + ts * 1024 + (2 * hp + h) * 64
                            parts.append((PS[2][64 * h:64 * h + 64, 0:128],
                                          VT[:, base:base + 64],
                                          PTS[:, j * 128:(j + 1) * 128],
                                          j == jl[0], j == jl[-1], (0, 64 * h)))
                        b.emit("pe", _mmg(parts), reads=tuple(vread) + ("PTS",),
                               writes=("ps2",))
                    b.emit("dve", lambda e, qc=qcols: e.tensor_mul(
                        O[:, qc:qc + 128], PS[2][:, 0:128], RBC[:, :]),
                        reads=("ps2", "RBC"), writes=("O",))

        wslot_load(0, woutA, 0, 8)
        for fo in range(8):
            s = fo % 2
            if fo + 1 < 8:
                wslot_load(fo + 1, woutA, fo + 1, 8)
            parts = [(PS[s][:, 0:512], WS[s][:, kk * 128:(kk + 1) * 128],
                      O[:, kk * 512:(kk + 1) * 512], kk == 0, kk == 7, None)
                     for kk in range(8)]
            b.emit("pe", _mmg(parts), reads=(f"WS{s}", "O"), writes=(psz[s],))
            b.emit("dve", lambda e, fo=fo, s=s: e.tensor_add(
                ACC[:, fo * 512:(fo + 1) * 512], PS[s][:, 0:512],
                X[:, fo * 512:(fo + 1) * 512]),
                reads=(psz[s], "X"), writes=(f"ACC{fo}",))
            b.dma(CO, acc_s[fo * 128:(fo + 1) * 128, ccol:ccol + 512],
                  ACC[:, fo * 512:(fo + 1) * 512],
                  reads=(f"ACC{fo}",), writes=("d:acc",))

    b.barrier()

    def sweep_ffn(l, src, dst):
        b.emit("dve", lambda e: e.memset(GXN[:, :], 0.0), writes=("GXN",))
        b.emit("dve", lambda e: e.memset(GX1[:, :], 0.0), writes=("GX1",))
        for c in range(NC):
            ccol = c * 512
            spv = (c - 1) % 2
            scv = c % 2
            first = (c == 0)
            if src == "x":
                for kk in range(8):
                    b.dma(CX, X[:, kk * 512:(kk + 1) * 512],
                          xT[kk * 128:(kk + 1) * 128, ccol:ccol + 512], writes=("X",))
                for kk in range(8):
                    b.emit("dve", lambda e, kk=kk: e.tensor_copy(
                        XN[:, kk * 513:kk * 513 + 1], GXN[:, kk:kk + 1]),
                        reads=("GXN",), writes=("XN",))
                emit_ln_stats_and_apply(
                    "X", lambda kk: X[:, kk * 512:(kk + 1) * 512], ("XN",),
                    dst_slices=lambda kk: XN[:, kk * 513 + 1:kk * 513 + 513])
                for kk in range(8):
                    b.emit("dve", lambda e, kk=kk: e.tensor_copy(
                        GXN[:, kk:kk + 1], XN[:, kk * 513 + 512:kk * 513 + 513]),
                        reads=("XN",), writes=("GXN",))
            else:
                for kk in range(8):
                    b.dma(CX, X[:, kk * 512:(kk + 1) * 512],
                          acc_s[kk * 128:(kk + 1) * 128, ccol:ccol + 512],
                          reads=("d:acc",), writes=("X",))
                if first:
                    for kk in range(8):
                        b.emit("dve", lambda e, kk=kk: e.memset(
                            XN[:, kk * 513:kk * 513 + 1], 0.0), writes=("XN",))
                        b.dma(CX, XN[:, kk * 513 + 1:kk * 513 + 513],
                              y1_s[kk * 128:(kk + 1) * 128, 0:512],
                              reads=("d:y1",), writes=("XN",))
                else:
                    for kk in range(8):
                        b.dma(CX, XN[:, kk * 513:kk * 513 + 513],
                              y1_s[kk * 128:(kk + 1) * 128, ccol - 1:ccol + 512],
                              reads=("d:y1",), writes=("XN",))
            for i in range(4):
                b.dma(CRT, RT[:, i * 512:(i + 1) * 512],
                      rott[l][i * 128:(i + 1) * 128, ccol:ccol + 512],
                      reads=(f"d:rott{l}",), writes=("RT",))

            def hshift(kk, off=0, w=512):
                base = kk * 513 + (1 if kk < 4 else 0) + off
                return XN[:, base:base + w]

            wslot_load(0, wqkvF[l], 0, 8)
            for m in range(12):
                s = m % 2
                if m + 1 < 12:
                    wslot_load(m + 1, wqkvF[l], m + 1, 8)
                parts = [(PS[s][:, 0:512], WS[s][:, kk * 128:(kk + 1) * 128],
                          hshift(kk), kk == 0, kk == 7, None) for kk in range(8)]
                b.emit("pe", _mmg(parts), reads=(f"WS{s}", "XN"), writes=(psz[s],))
                if m < 8:
                    b.emit("act", lambda e, s=s: e.activation(SQA[:, :], PS[s][:, 0:512],
                                                              AF.Square),
                           reads=(psz[s],), writes=("SQA",))
                    b.emit("pe", _mmg([(PS[5][0:1, 0:512], ONES2[:, 0:1], SQA[:, :],
                                        True, True, None),
                                       (PS[6][0:1, 0:512], ONES2[:, 1:2], SQA[:, :],
                                        True, True, None)]),
                           reads=("SQA", "ONES2"), writes=("ps5", "ps6"))
                    b.emit("dve", lambda e: e.tensor_scalar_max(RSS[0:1, :],
                                                                PS[5][0:1, 0:512], 1e-24),
                           reads=("ps5",), writes=("RSS",))
                    b.emit("dve", lambda e: e.tensor_scalar_max(P[0:1, 0:512],
                                                                PS[6][0:1, 0:512], 1e-24),
                           reads=("ps6",), writes=("P",))
                    b.emit("act", lambda e: e.activation(RSS[0:1, :], RSS[0:1, :], AF.Ln),
                           reads=(), writes=("RSS",))
                    b.emit("act", lambda e: e.activation(RSS[0:1, :], RSS[0:1, :], AF.Exp,
                                                         scale=-0.5),
                           reads=(), writes=("RSS",))
                    b.emit("act", lambda e: e.activation(P[0:1, 0:512], P[0:1, 0:512], AF.Ln),
                           reads=(), writes=("P",))
                    b.emit("act", lambda e: e.activation(P[0:1, 0:512], P[0:1, 0:512], AF.Exp,
                                                         scale=-0.5),
                           reads=(), writes=("P",))
                    b.emit("pe", _mmg([(PS[6][0:64, 0:512], BONE[0:1, 0:64],
                                        RSS[0:1, :], True, True, (0, 0)),
                                       (PS[6][64:128, 0:512], BONE[0:1, 0:64],
                                        P[0:1, 0:512], True, True, (0, 64))]),
                           reads=("RSS", "P", "BONE"), writes=("ps6",))
                    b.emit("act", lambda e: e.copy(MB[:, :], PS[6][:, 0:512]),
                           reads=("ps6",), writes=("MB",))
                    b.emit("dve", lambda e, s=s: e.tensor_mul(QR[:, :], PS[s][:, 0:512],
                                                              MB[:, :]),
                           reads=(psz[s], "MB"), writes=("QR",))
                    tb = 0 if m < 4 else 1024
                    if m < 4:
                        dzone = "Q"
                        dcol0, dcol1 = m * 512, (m + 1) * 512

                        def dsl(p0, p1, a=dcol0, bcol=dcol1):
                            return BIGQ[p0:p1, a:bcol]
                    else:
                        j = m - 4
                        dzone = f"KR{scv}"
                        dcol0 = j * 1024 + scv * 512
                        dcol1 = dcol0 + 512

                        def dsl(p0, p1, a=dcol0, bcol=dcol1):
                            return KR[p0:p1, a:bcol]
                    b.emit("pe", _mmg([(PS[5][:, 0:512], RROT[:, :], QR[:, :],
                                        True, True, None)]),
                           reads=("RROT", "QR"), writes=("ps5",))
                    b.emit("dve", lambda e, tb=tb, d=dsl(0, 128): e.tensor_mul(
                        d, QR[:, :], RT[:, tb:tb + 512]),
                        reads=("QR", "RT"), writes=(dzone,))
                    b.emit("dve", lambda e, tb=tb: e.tensor_mul(
                        VS[:, :], PS[5][:, 0:512], RT[:, tb + 512:tb + 1024]),
                        reads=("ps5", "RT"), writes=("VS",))
                    b.emit("dve", lambda e, d=dsl(0, 128): e.tensor_add(d, d, VS[:, :]),
                           reads=("VS",), writes=(dzone,))
                else:
                    j = m - 8
                    b.emit("act", lambda e, s=s: e.copy(VS[:, :], PS[s][:, 0:512]),
                           reads=(psz[s],), writes=("VS",))
                    for u in range(4):
                        b.emit("pe", lambda e, u=u: e.transpose(
                            PS[2][:, u * 128:(u + 1) * 128],
                            VS[:, u * 128:(u + 1) * 128], IDENT[:, :]),
                            reads=("VS", "IDENT"), writes=("ps2",))
                    dstv = bass.AP(VT, scv * 2048 + j * 128,
                                   [[6144, 128], [512, 4], [1, 128]])
                    b.emit("act", lambda e, d=dstv: e.copy(d, PS[2][:, 0:512]),
                           reads=("ps2",), writes=(f"VT{scv}",))

            for hp in range(4):
                for qs in range(4):
                    qcols = hp * 512 + qs * 128
                    scidx0 = (hp * 4 + qs) * 2
                    for h in range(2):
                        scidx = scidx0 + h
                        bkA = 2 + 2 * h
                        bkB = 3 + 2 * h
                        if not first:
                            b.emit("pe", _mmg([(PS[bkA][:, 0:512],
                                                Qa(qcols, qcols + 128, 64 * h, 64 * h + 64),
                                                KR[64 * h:64 * h + 64,
                                                   hp * 1024 + spv * 512:hp * 1024 + spv * 512 + 512],
                                                True, True, None)]),
                                   reads=("Q", f"KR{spv}"), writes=(psz[bkA],))
                        b.emit("pe", _mmg([(PS[bkB][:, 0:512],
                                            Qa(qcols, qcols + 128, 64 * h, 64 * h + 64),
                                            KR[64 * h:64 * h + 64,
                                               hp * 1024 + scv * 512:hp * 1024 + scv * 512 + 512],
                                            True, True, None)]),
                               reads=("Q", f"KR{scv}"), writes=(psz[bkB],))
                        if not first:
                            b.emit("dve", lambda e, bk=bkA, qs=qs: e.tensor_add(
                                PS[bk][:, 0:512], PS[bk][:, 0:512],
                                MF[:, qs * 1024:qs * 1024 + 512]),
                                reads=("MF",), writes=(psz[bkA],))
                        b.emit("dve", lambda e, bk=bkB, qs=qs: e.tensor_add(
                            PS[bk][:, 0:512], PS[bk][:, 0:512],
                            MF[:, qs * 1024 + 512:qs * 1024 + 1024]),
                            reads=("MF",), writes=(psz[bkB],))
                        if not first:
                            b.emit("dve", lambda e, bk=bkA: e.reduce_max(
                                NMT[:, 0:1], PS[bk][:, 0:512], axis=AX.X, negate=True),
                                reads=(psz[bkA],), writes=("NMT",))
                        b.emit("dve", lambda e, bk=bkB: e.reduce_max(
                            NMT[:, 1:2], PS[bk][:, 0:512], axis=AX.X, negate=True),
                            reads=(psz[bkB],), writes=("NMT",))
                        if not first:
                            b.emit("dve", lambda e: e.tensor_tensor(
                                NMT[:, 4:5], NMT[:, 0:1], NMT[:, 1:2], ALU.min),
                                reads=(), writes=("NMT",))
                            b.emit("dve", lambda e: e.tensor_scalar_mul(
                                NMT[:, 6:7], NMT[:, 4:5], 8.0),
                                reads=(), writes=("NMT",))
                        else:
                            b.emit("dve", lambda e: e.tensor_scalar_mul(
                                NMT[:, 6:7], NMT[:, 1:2], 8.0),
                                reads=(), writes=("NMT",))
                        if not first:
                            b.emit("act", lambda e, bk=bkA, si=scidx: e.activation(
                                P[:, 0:512], PS[bk][:, 0:512], AF.Exp,
                                bias=NMT[:, 6:7], scale=8.0,
                                accum_out=SM[:, 2 * si:2 * si + 1]),
                                reads=(psz[bkA], "NMT"), writes=("P", "SM"))
                            b.emit("act", lambda e, bk=bkB, si=scidx: e.activation(
                                P[:, 512:1024], PS[bk][:, 0:512], AF.Exp,
                                bias=NMT[:, 6:7], scale=8.0,
                                accum_out=SM[:, 2 * si + 1:2 * si + 2]),
                                reads=(psz[bkB], "NMT"), writes=("P", "SM"))
                            b.emit("dve", lambda e, si=scidx: e.tensor_add(
                                SM[:, 64 + si:65 + si], SM[:, 2 * si:2 * si + 1],
                                SM[:, 2 * si + 1:2 * si + 2]),
                                reads=(), writes=("SM",))
                        else:
                            b.emit("act", lambda e, bk=bkB, si=scidx: e.activation(
                                P[:, 512:1024], PS[bk][:, 0:512], AF.Exp,
                                bias=NMT[:, 6:7], scale=8.0,
                                accum_out=SM[:, 64 + si:65 + si]),
                                reads=(psz[bkB], "NMT"), writes=("P", "SM"))
                        jl = [4, 5, 6, 7] if first else list(range(8))
                        for j in jl:
                            pb = 6 + j // 4
                            b.emit("pe", lambda e, j=j, pb=pb: e.transpose(
                                PS[pb][:, (j % 4) * 128:(j % 4) * 128 + 128],
                                P[:, j * 128:(j + 1) * 128], IDENT[:, :]),
                                reads=("P", "IDENT"), writes=(psz[pb],))
                        if not first:
                            b.emit("act", lambda e: e.copy(PTS[:, 0:512], PS[6][:, 0:512]),
                                   reads=("ps6",), writes=("PTS",))
                        b.emit("act", lambda e: e.copy(PTS[:, 512:1024], PS[7][:, 0:512]),
                               reads=("ps7",), writes=("PTS",))
                        parts = []
                        vread = set()
                        for j in jl:
                            sl = spv if j < 4 else scv
                            ts = j % 4
                            vread.add(f"VT{sl}")
                            base = sl * 2048 + ts * 512 + (2 * hp + h) * 64
                            parts.append((PS[0][64 * h:64 * h + 64, 0:128],
                                          VT[:, base:base + 64],
                                          PTS[:, j * 128:(j + 1) * 128],
                                          j == jl[0], j == jl[-1], (0, 64 * h)))
                        b.emit("pe", _mmg(parts), reads=tuple(vread) + ("PTS",),
                               writes=("ps0",))
                    b.emit("pe", lambda e, si=scidx0: [
                        e.transpose(PS[1][0:1, 0:128], SM[:, 64 + si:65 + si], IDENT[:, :]),
                        e.transpose(PS[1][0:1, 128:256], SM[:, 65 + si:66 + si], IDENT[:, :])],
                        reads=("SM", "IDENT"), writes=("ps1",))
                    b.emit("dve", lambda e: e.reciprocal(SMD[0:1, :], PS[1][0:1, 0:256]),
                           reads=("ps1",), writes=("SMD",))
                    b.emit("pe", _mmg([(PS[1][0:64, 0:128], BONE[0:1, 0:64],
                                        SMD[0:1, 0:128], True, True, (0, 0)),
                                       (PS[1][64:128, 0:128], BONE[0:1, 0:64],
                                        SMD[0:1, 128:256], True, True, (0, 64))]),
                           reads=("SMD", "BONE"), writes=("ps1",))
                    b.emit("act", lambda e: e.copy(RBC[:, :], PS[1][:, 0:128]),
                           reads=("ps1",), writes=("RBC",))
                    b.emit("dve", lambda e, qc=qcols: e.tensor_mul(
                        O[:, qc:qc + 128], PS[0][:, 0:128], RBC[:, :]),
                        reads=("ps0", "RBC"), writes=("O",))

            wslot_load(0, woF[l], 0, 4)
            for fo in range(8):
                s = fo % 2
                if fo + 1 < 8:
                    wslot_load(fo + 1, woF[l], fo + 1, 4)
                parts = [(PS[s][:, 0:512], WS[s][:, kk * 128:(kk + 1) * 128],
                          O[:, kk * 512:(kk + 1) * 512], kk == 0, kk == 3, None)
                         for kk in range(4)]
                b.emit("pe", _mmg(parts), reads=(f"WS{s}", "O"), writes=(psz[s],))
                b.emit("dve", lambda e, fo=fo, s=s: e.tensor_add(
                    X1s(fo * 513 + 1, fo * 513 + 513), PS[s][:, 0:512],
                    XN[:, fo * 513 + 1:fo * 513 + 513]),
                    reads=(psz[s], "XN"), writes=("X1",))
            for kk in range(8):
                b.emit("dve", lambda e, kk=kk: e.tensor_copy(
                    X1s(kk * 513, kk * 513 + 1), GX1[:, kk:kk + 1]),
                    reads=("GX1",), writes=("X1",))
            for kk in range(8):
                b.emit("dve", lambda e, kk=kk: e.tensor_copy(
                    GX1[:, kk:kk + 1], X1s(kk * 513 + 512, kk * 513 + 513)),
                    reads=("X1",), writes=("GX1",))

            def h2slice(kk):
                base = kk * 513 + (1 if kk < 4 else 0)
                return X1s(base, base + 512)

            emit_ln_stats_and_apply(
                "X1", h2slice, ("XN",),
                dst_slices=lambda kk: XN[:, kk * 513 + 1:kk * 513 + 513],
                affine=(lambda kk: LNG[l][:, kk:kk + 1], lambda kk: LNB[l][:, kk:kk + 1]))

            for ts2 in range(2):
                tw = ts2 * 256
                wslot_load(ts2 * 22, wfus[l], 0, 24)
                for m in range(22):
                    s = (ts2 * 22 + m) % 2
                    if m + 1 < 22:
                        wslot_load(ts2 * 22 + m + 1, wfus[l], m + 1, 24)
                    sa = m % 2
                    parts = [(PS[sa][:, 0:256], WS[s][:, kk * 128:(kk + 1) * 128],
                              XN[:, kk * 513 + 1 + tw:kk * 513 + 1 + tw + 256],
                              kk == 0, False, None) for kk in range(8)]
                    parts += [(PS[sa][:, 256:512], WS[s][:, (8 + kk) * 128:(9 + kk) * 128],
                               XN[:, kk * 513 + 1 + tw:kk * 513 + 1 + tw + 256],
                               False, kk == 7, None) for kk in range(8)]
                    b.emit("pe", _mmg(parts), reads=(f"WS{s}", "XN"), writes=(psz[sa],))
                    b.emit("act", lambda e, sa=sa, m=m: e.activation(
                        MG[:, :], PS[sa][:, 256:512], AF.Gelu,
                        bias=B1G[l][:, m:m + 1]),
                        reads=(psz[sa], "BI"), writes=("MG",))
                    b.emit("dve", lambda e, sa=sa, m=m: e.scalar_tensor_tensor(
                        Mt[:, :], PS[sa][:, 0:256], B1A[l][:, m:m + 1], MG[:, :],
                        ALU.add, ALU.mult),
                        reads=(psz[sa], "MG", "BI"), writes=("M",))
                    parts = [(PS[2 + fo // 2][:, (fo % 2) * 256:(fo % 2) * 256 + 256],
                              WS[s][:, (16 + fo) * 128:(17 + fo) * 128],
                              Mt[:, :], (m == 0 and fo % 2 == 0),
                              (m == 21 and fo % 2 == 1), None)
                             for fo in range(8)]
                    b.emit("pe", _mmg(parts), reads=(f"WS{s}", "M"),
                           writes=("ps2", "ps3", "ps4", "ps5"))
                for fo in range(8):
                    w2v = PS[2 + fo // 2][:, (fo % 2) * 256:(fo % 2) * 256 + 256]
                    b.emit("dve", lambda e, fo=fo, w2v=w2v, tw=tw: e.scalar_tensor_tensor(
                        ACC[:, fo * 512 + tw:fo * 512 + tw + 256], w2v,
                        B2[l][:, fo:fo + 1],
                        X1s(fo * 513 + 1 + tw, fo * 513 + 1 + tw + 256),
                        ALU.add, ALU.add),
                        reads=("ps" + str(2 + fo // 2), "X1", "BI"),
                        writes=(f"ACC{fo}",))
                    if dst == "out":
                        b.emit("dve", lambda e, fo=fo, tw=tw: e.tensor_add(
                            ACC[:, fo * 512 + tw:fo * 512 + tw + 256],
                            ACC[:, fo * 512 + tw:fo * 512 + tw + 256],
                            X[:, fo * 512 + tw:fo * 512 + tw + 256]),
                            reads=("X",), writes=(f"ACC{fo}",))
            dstd = y1_s if dst == "y1" else outT
            dzone = "d:y1" if dst == "y1" else "d:out"
            for fo in range(8):
                b.dma(CO, dstd[fo * 128:(fo + 1) * 128, ccol:ccol + 512],
                      ACC[:, fo * 512:(fo + 1) * 512],
                      reads=(f"ACC{fo}",), writes=(dzone,))

    sweep_ffn(0, "x", "y1")
    b.barrier()
    sweep_ffn(1, "y1", "out")

    b.wait_all_chans("sp")
    b.replay()
    nc.finalize()
    return nc



def _tile_w(w, nk, nm):
    K, M = w.shape
    assert K == nk * 128 and M == nm * 128
    t = w.reshape(nk, 128, nm, 128).transpose(2, 1, 0, 3)
    return np.ascontiguousarray(t).reshape(nm, 128 * nk * 128)


def _pad_cols(w, M):
    out = np.zeros((w.shape[0], M), dtype=w.dtype)
    out[:, :w.shape[1]] = w
    return out


def _pad_rows(w, K):
    out = np.zeros((K, w.shape[1]), dtype=w.dtype)
    out[:w.shape[0]] = w
    return out


def stage_inputs(L0_attn_w_qkv, L0_attn_w_out, L0_ffn_wqkv, L0_ffn_q_scale,
                 L0_ffn_k_scale, L0_ffn_wo, L0_ffn_ln_g, L0_ffn_ln_b,
                 L0_ffn_w1, L0_ffn_b1, L0_ffn_w2, L0_ffn_b2):
    f = np.float32
    d = {}
    d["wqkvA"] = _tile_w(np.asarray(L0_attn_w_qkv, f), 8, 24)
    d["woutA"] = _tile_w(np.asarray(L0_attn_w_out, f), 8, 8)
    for l in range(2):
        d[f"wqkvF{l}"] = _tile_w(np.asarray(L0_ffn_wqkv[l], f), 8, 12)
        d[f"woF{l}"] = _tile_w(np.asarray(L0_ffn_wo[l], f), 4, 8)
        w1 = np.asarray(L0_ffn_w1[l], f)
        wa = _pad_cols(w1[:, :2730], 2816).reshape(8, 128, 22, 128).transpose(2, 1, 0, 3)
        wg = _pad_cols(w1[:, 2730:], 2816).reshape(8, 128, 22, 128).transpose(2, 1, 0, 3)
        w2 = _pad_rows(np.asarray(L0_ffn_w2[l], f), 2816).reshape(22, 128, 8, 128)
        d[f"wfus{l}"] = np.ascontiguousarray(
            np.concatenate([wa, wg, w2], axis=2)).reshape(22, 128 * 24 * 128)
        b1 = np.asarray(L0_ffn_b1[l], f)
        d[f"b1a{l}"] = np.ascontiguousarray(
            np.concatenate([b1[:2730], np.zeros(86, f)]).reshape(22, 128).T)
        d[f"b1g{l}"] = np.ascontiguousarray(
            np.concatenate([b1[2730:], np.zeros(86, f)]).reshape(22, 128).T)
        d[f"b2{l}"] = np.ascontiguousarray(np.asarray(L0_ffn_b2[l], f).reshape(8, 128).T)
        d[f"lng{l}"] = np.ascontiguousarray(np.asarray(L0_ffn_ln_g[l], f).reshape(8, 128).T)
        d[f"lnb{l}"] = np.ascontiguousarray(np.asarray(L0_ffn_ln_b[l], f).reshape(8, 128).T)
        qs = np.asarray(L0_ffn_q_scale[l], f)
        ks = np.asarray(L0_ffn_k_scale[l], f)
        d[f"qsc{l}"] = np.ascontiguousarray(
            np.stack([np.tile(qs, 2), np.tile(np.roll(qs, -32), 2)], axis=1))
        d[f"ksc{l}"] = np.ascontiguousarray(
            np.stack([np.tile(ks, 2), np.tile(np.roll(ks, -32), 2)], axis=1))
    return d


def kernel(L0_attn_x, L0_attn_w_qkv, L0_attn_w_out, L0_ffn_wqkv, L0_ffn_q_scale,
           L0_ffn_k_scale, L0_ffn_wo, L0_ffn_ln_g, L0_ffn_ln_b, L0_ffn_w1,
           L0_ffn_b1, L0_ffn_w2, L0_ffn_b2):
    x = np.asarray(L0_attn_x, np.float32)
    B, NT, D = x.shape
    nc = build_core_program(NT)
    wd = stage_inputs(L0_attn_w_qkv, L0_attn_w_out, L0_ffn_wqkv, L0_ffn_q_scale,
                      L0_ffn_k_scale, L0_ffn_wo, L0_ffn_ln_g, L0_ffn_ln_b,
                      L0_ffn_w1, L0_ffn_b1, L0_ffn_w2, L0_ffn_b2)
    in_maps = []
    for bb in range(B):
        m = {"xT": np.ascontiguousarray(x[bb].T)}
        m.update(wd)
        in_maps.append(m)
    res = run_bass_kernel_spmd(nc, in_maps, core_ids=list(range(B)))
    out = np.stack([np.asarray(r["outT"]).T for r in res.results], axis=0)
    return out.astype(np.float32)


# version 06
# speedup vs baseline: 1.0350x; 1.0350x over previous
import math

import numpy as np

import concourse.bass as bass
import concourse.mybir as mybir
from concourse.bass_utils import run_bass_kernel_spmd
import concourse.library_config as libcfg

F32 = mybir.dt.float32
BF16 = mybir.dt.bfloat16
AF = mybir.ActivationFunctionType
ALU = mybir.AluOpType
AX = mybir.AxisListType

T = 512
NEG = -1.0e30
PI = math.pi
LN1E4 = math.log(10000.0)

ENGS = ("pe", "dve", "act", "pool", "sp")
_BLOCKSECT = {"pe": "tensor", "dve": "vector", "act": "scalar",
              "pool": "gpsimd", "sp": "sync"}

PROXY_LIB = [l for l in libcfg.all_libraries if l.name == "proxy"][0]


class Chan:
    __slots__ = ("name", "sem", "issued")

    def __init__(self, name, sem):
        self.name = name
        self.sem = sem
        self.issued = 0


class Bld:

    def __init__(self, nc):
        self.nc = nc
        self.ops = {e: [] for e in ENGS}
        self.sem = {e: nc.alloc_semaphore("tk_" + e) for e in ENGS}
        self.n = {e: 0 for e in ENGS}
        self.wm = {e: {} for e in ENGS}
        self.zw = {}
        self.zr = {}
        self.chans = []

    def chan(self, name):
        c = Chan(name, self.nc.alloc_semaphore("ch_" + name))
        self.chans.append(c)
        return c

    def _wait(self, e, key, val):
        if isinstance(key, Chan):
            val = 16 * key.issued
        if val <= 0:
            return
        if self.wm[e].get(key, 0) >= val:
            return
        self.wm[e][key] = val
        semh = key.sem if isinstance(key, Chan) else self.sem[key]
        self.ops[e].append(("w", semh, val))

    def _dep(self, e, zones, war):
        for z in zones:
            lw = self.zw.get(z)
            if lw is not None:
                self._wait(e, lw[0], lw[1])
            if war:
                for k, v in self.zr.get(z, {}).items():
                    self._wait(e, k, v)

    def emit(self, e, fn, reads=(), writes=()):
        self._dep(e, reads, False)
        self._dep(e, writes, True)
        self.ops[e].append(("i", fn, self.sem[e]))
        self.n[e] += 1
        t = self.n[e]
        for z in reads:
            self.zr.setdefault(z, {})[e] = t
        for z in writes:
            self.zw[z] = (e, t)
            self.zr[z] = {}

    def dma(self, ch, out_ap, in_ap, reads=(), writes=(), eng="sp"):
        self._dep(eng, reads, False)
        self._dep(eng, writes, True)
        self.ops[eng].append(("d", out_ap, in_ap, ch.sem))
        ch.issued += 1
        for z in reads:
            self.zr.setdefault(z, {})[ch] = 0
        for z in writes:
            self.zw[z] = (ch, 0)
            self.zr[z] = {}

    def barrier(self):
        for e1 in ENGS:
            for e2 in ENGS:
                if e1 != e2:
                    self._wait(e1, e2, self.n[e2])

    def wait_all_chans(self, e="sp"):
        for c in self.chans:
            self._wait(e, c, 1)

    def replay(self):
        with self.nc.Block() as block:
            for e in ENGS:
                ops = self.ops[e]

                def body(eng, ops=ops):
                    for op in ops:
                        if op[0] == "w":
                            eng.wait_ge(op[1], op[2])
                        elif op[0] == "i":
                            r = op[1](eng)
                            if isinstance(r, list):
                                r = r[-1]
                            r.then_inc(op[2], 1)
                        else:
                            eng.dma_start(op[1], op[2]).then_inc(op[3], 16)
                getattr(block, _BLOCKSECT[e])(body)


def _mmg(parts):
    def fn(eng):
        last = None
        for (o, l, r, s, st, tp) in parts:
            if tp is None:
                last = eng.matmul(o, l, r, start=s, stop=st)
            else:
                last = eng.matmul(o, l, r, start=s, stop=st, tile_position=tp)
        return last
    return fn


def build_core_program(NT=4096):
    NC = NT // T
    nc = bass.Bass("TRN2", target_bir_lowering=False)

    xT = nc.dram_tensor("xT", [1024, NT], F32, kind="ExternalInput").ap()
    wqkvA = nc.dram_tensor("wqkvA", [24, 128 * 8 * 128], F32, kind="ExternalInput").ap()
    woutA = nc.dram_tensor("woutA", [8, 128 * 8 * 128], F32, kind="ExternalInput").ap()
    wqkvF = [nc.dram_tensor(f"wqkvF{l}", [12, 128 * 8 * 128], F32, kind="ExternalInput").ap() for l in range(2)]
    woF = [nc.dram_tensor(f"woF{l}", [8, 128 * 4 * 128], F32, kind="ExternalInput").ap() for l in range(2)]
    wfus = [nc.dram_tensor(f"wfus{l}", [22, 128 * 24 * 128], F32, kind="ExternalInput").ap() for l in range(2)]
    b1a_d = [nc.dram_tensor(f"b1a{l}", [128, 22], F32, kind="ExternalInput").ap() for l in range(2)]
    b1g_d = [nc.dram_tensor(f"b1g{l}", [128, 22], F32, kind="ExternalInput").ap() for l in range(2)]
    b2_d = [nc.dram_tensor(f"b2{l}", [128, 8], F32, kind="ExternalInput").ap() for l in range(2)]
    lng_d = [nc.dram_tensor(f"lng{l}", [128, 8], F32, kind="ExternalInput").ap() for l in range(2)]
    lnb_d = [nc.dram_tensor(f"lnb{l}", [128, 8], F32, kind="ExternalInput").ap() for l in range(2)]
    qsc_d = [nc.dram_tensor(f"qsc{l}", [128, 2], F32, kind="ExternalInput").ap() for l in range(2)]
    ksc_d = [nc.dram_tensor(f"ksc{l}", [128, 2], F32, kind="ExternalInput").ap() for l in range(2)]
    outT = nc.dram_tensor("outT", [1024, NT], F32, kind="ExternalOutput").ap()
    acc_s = nc.dram_tensor("acc_s", [1024, NT], F32, kind="Internal").ap()
    y1_s = nc.dram_tensor("y1_s", [1024, NT], F32, kind="Internal").ap()
    rott = [nc.dram_tensor(f"rott{l}", [4 * 128, NT], F32, kind="Internal").ap() for l in range(2)]

    A = nc.alloc_sbuf_tensor
    XN = A("XN", [128, 8 * 513], F32)
    X = A("Xt", [128, 8 * 512], F32)
    BIGQ = A("BIGQ", [128, 6152], F32)
    X1OFF = 2048
    ACC = A("ACCt", [128, 8 * 512], F32)
    O = A("Ot", [128, 8 * 512], F32)
    KR = A("KRt", [128, 6144], F32)
    VT = A("VTt", [128, 6144], F32)
    WS = [A("WS0", [128, 24 * 128], F32), A("WS1", [128, 24 * 128], F32)]
    P = A("Pt", [128, 1024], F32)
    PTS = A("PTSt", [128, 1024], F32)
    Mt = A("Mt", [128, 256], F32)
    MG = A("MGt", [128, 256], F32)
    QR = A("QRt", [128, 512], F32)
    VS = A("VSt", [128, 512], F32)
    SQA = A("SQAt", [128, 512], F32)
    SQB = A("SQBt", [128, 512], F32)
    MB = A("MBt", [128, 512], F32)
    RB = A("RBt", [128, 512], F32)
    RT = A("RTt", [128, 2048], F32)
    MA = A("MAt", [128, 1024], BF16)
    MF = A("MFt", [128, 4096], BF16)
    SM = A("SMt", [128, 96], F32)
    NMT = A("NMTt", [128, 16], F32)
    SMD = A("SMDt", [1, 256], F32)
    RBC = A("RBCt", [128, 128], F32)
    IDENT = A("IDENTt", [128, 128], F32)
    ONES = A("ONESt", [128, 1], F32)
    ONES2 = A("ONES2t", [128, 2], F32)
    GXN = A("GXNt", [128, 8], F32)
    GX1 = A("GX1t", [128, 8], F32)
    RSS = A("RSSt", [1, 512], F32)
    RROT = A("RROTt", [128, 128], F32)
    BONE = A("BONEt", [1, 128], F32)
    CEPS = A("CEPSt", [128, 1], F32)
    IOD = A("IODt", [128, 4], F32)
    B1A = [A(f"B1A{l}", [128, 22], F32) for l in range(2)]
    B1G = [A(f"B1G{l}", [128, 22], F32) for l in range(2)]
    B2 = [A(f"B2{l}", [128, 8], F32) for l in range(2)]
    LNG = [A(f"LNG{l}", [128, 8], F32) for l in range(2)]
    LNB = [A(f"LNB{l}", [128, 8], F32) for l in range(2)]
    QSC = [A(f"QSC{l}", [128, 2], F32) for l in range(2)]
    KSC = [A(f"KSC{l}", [128, 2], F32) for l in range(2)]

    def Qa(a, b_, p0=0, p1=128):
        return BIGQ[p0:p1, a:b_]

    def X1s(a, b_):
        return BIGQ[:, X1OFF + a:X1OFF + b_]

    PS = [nc.alloc_psum_tensor(f"PS{i}", [128, 512], F32) for i in range(8)]

    b = Bld(nc)
    CX = b.chan("cx")
    CW = [b.chan("cw0"), b.chan("cw1")]
    CRT = b.chan("crt")
    CB = b.chan("cb")
    CO = b.chan("co")
    CTW = b.chan("ctw")

    psz = [f"ps{i}" for i in range(8)]

    b.emit("dve", lambda e: e.memset(BONE[:, :], 1.0), writes=("BONE",))

    b.emit("dve", lambda e: e.memset(IDENT[:, :], 1.0), writes=("IDENT",))
    b.emit("pool", lambda e: e.affine_select(IDENT[:, :], IDENT[:, :], [[1, 128]],
                                             ALU.is_equal, 0.0, base=0,
                                             channel_multiplier=-1),
           reads=("lib",), writes=("IDENT",))
    b.emit("dve", lambda e: e.memset(ONES[:, :], 1.0), writes=("ONES",))
    b.emit("dve", lambda e: e.memset(CEPS[:, :], 1e-5), writes=("CEPS",))
    b.emit("dve", lambda e: e.memset(ONES2[:, :], 1.0), writes=("ONES2",))
    b.emit("pool", lambda e: e.affine_select(ONES2[:, 0:1], ONES2[:, 0:1], [[1, 1]],
                                             ALU.is_ge, 0.0, base=63,
                                             channel_multiplier=-1),
           reads=("lib",), writes=("ONES2",))
    b.emit("pool", lambda e: e.affine_select(ONES2[:, 1:2], ONES2[:, 1:2], [[1, 1]],
                                             ALU.is_ge, 0.0, base=-64,
                                             channel_multiplier=1),
           reads=("lib",), writes=("ONES2",))
    for (c0, sgn, kb) in ((0, -1.0, -32), (32, 1.0, 0), (64, -1.0, -96), (96, 1.0, -64)):
        sl = RROT[:, c0:c0 + 32]
        b.emit("dve", lambda e, sl=sl, sgn=sgn: e.memset(sl, sgn), writes=("RROT",))
        b.emit("pool", lambda e, sl=sl, kb=kb: e.affine_select(
            sl, sl, [[-1, 32]], ALU.is_equal, 0.0, base=kb,
            channel_multiplier=1), reads=("lib",), writes=("RROT",))
    b.emit("dve", lambda e: e.memset(MA[:, :], 0.0), writes=("MA",))
    b.emit("dve", lambda e: e.memset(MF[:, :], 0.0), writes=("MF",))
    for qs in range(2):
        sl = MA[:, qs * 512:(qs + 1) * 512]
        b.emit("pool", lambda e, sl=sl, qs=qs: e.affine_select(
            sl, sl, [[1, 512]], ALU.is_ge, NEG, base=-(128 * qs) - 1,
            channel_multiplier=-1), reads=("lib",), writes=("MA",))
        b.emit("pool", lambda e, sl=sl, qs=qs: e.affine_select(
            sl, sl, [[-1, 512]], ALU.is_ge, NEG, base=(128 * qs) + 256,
            channel_multiplier=1), reads=("lib",), writes=("MA",))
    for qs in range(4):
        sl = MF[:, qs * 1024:(qs + 1) * 1024]
        b.emit("pool", lambda e, sl=sl, qs=qs: e.affine_select(
            sl, sl, [[1, 1024]], ALU.is_ge, NEG, base=-(128 * qs),
            channel_multiplier=-1), reads=("lib",), writes=("MF",))
        b.emit("pool", lambda e, sl=sl, qs=qs: e.affine_select(
            sl, sl, [[-1, 1024]], ALU.is_ge, NEG, base=(128 * qs) + 512,
            channel_multiplier=1), reads=("lib",), writes=("MF",))
    for l in range(2):
        b.dma(CB, B1A[l][:, :], b1a_d[l], writes=("BI",))
        b.dma(CB, B1G[l][:, :], b1g_d[l], writes=("BI",))
        b.dma(CB, B2[l][:, :], b2_d[l], writes=("BI",))
        b.dma(CB, LNG[l][:, :], lng_d[l], writes=("BI",))
        b.dma(CB, LNB[l][:, :], lnb_d[l], writes=("BI",))
        b.dma(CB, QSC[l][:, :], qsc_d[l], writes=("BI",))
        b.dma(CB, KSC[l][:, :], ksc_d[l], writes=("BI",))

    for qtr in range(4):
        b.emit("pool", lambda e, q=qtr: e.iota(
            IOD[q * 32:(q + 1) * 32, 0:1], [[1, 1]], base=0, channel_multiplier=1,
            allow_small_or_imprecise_dtypes=True), reads=("lib",), writes=("IOD",))
    b.emit("act", lambda e: e.activation(IOD[:, 1:2], IOD[:, 0:1], AF.Exp,
                                         scale=-2.0 * LN1E4 / 64.0),
           reads=("IOD",), writes=("IOD",))
    b.emit("dve", lambda e: e.tensor_scalar(IOD[:, 2:3], IOD[:, 0:1],
                                            2.0 / 89.6, 25.6 / 89.6,
                                            ALU.mult, ALU.add),
           reads=("IOD",), writes=("IOD",))
    b.emit("act", lambda e: e.activation(IOD[:, 3:4], IOD[:, 2:3], AF.Ln),
           reads=("IOD",), writes=("IOD",))
    for l in range(2):
        for cb in range(NC):
            t0 = cb * 512
            b.emit("pool", lambda e, t0=t0: e.iota(
                MB[:, :], [[1, 512]], base=t0, channel_multiplier=0,
                allow_small_or_imprecise_dtypes=True),
                reads=("lib",), writes=("MB",))
            b.emit("dve", lambda e: e.tensor_scalar_mul(RB[:, :], MB[:, :], IOD[:, 1:2]),
                   reads=("MB", "IOD"), writes=("RB",))
            b.emit("dve", lambda e: e.tensor_scalar_add(P[:, 0:512], RB[:, :], PI),
                   reads=("RB",), writes=("P",))
            for j in range(9, -1, -1):
                cj = 2.0 * PI * (1 << j)
                b.emit("dve", lambda e, cj=cj: e.tensor_single_scalar(
                    VS[:, :], P[:, 0:512], cj, ALU.is_ge),
                    reads=("P",), writes=("VS",))
                b.emit("dve", lambda e, cj=cj: e.scalar_tensor_tensor(
                    P[:, 0:512], VS[:, :], -cj, P[:, 0:512], ALU.mult, ALU.add),
                    reads=("VS",), writes=("P",))
            b.emit("dve", lambda e: e.tensor_scalar_add(P[:, 0:512], P[:, 0:512], -PI),
                   reads=(), writes=("P",))
            b.emit("dve", lambda e: e.tensor_scalar_add(P[:, 512:1024], P[:, 0:512], PI / 2),
                   reads=(), writes=("P",))
            b.emit("dve", lambda e: e.tensor_single_scalar(
                VS[:, :], P[:, 512:1024], PI, ALU.is_ge),
                reads=("P",), writes=("VS",))
            b.emit("dve", lambda e: e.scalar_tensor_tensor(
                P[:, 512:1024], VS[:, :], -2.0 * PI, P[:, 512:1024],
                ALU.mult, ALU.add),
                reads=("VS",), writes=("P",))
            b.emit("act", lambda e: e.activation(SQB[:, :], P[:, 0:512], AF.Sin),
                   reads=("P",), writes=("SQB",))
            b.emit("act", lambda e: e.activation(VS[:, :], P[:, 512:1024], AF.Sin),
                   reads=("P",), writes=("VS",))
            b.emit("pool", lambda e, t0=t0: e.iota(
                QR[:, :], [[1, 512]], base=t0 - 2048, channel_multiplier=0,
                allow_small_or_imprecise_dtypes=True),
                reads=("lib",), writes=("QR",))
            b.emit("dve", lambda e: e.tensor_scalar_mul(QR[:, :], QR[:, :], IOD[:, 3:4]),
                   reads=("IOD",), writes=("QR",))
            b.emit("act", lambda e: e.activation(MB[:, :], QR[:, :], AF.Exp,
                                                 scale=1.0 / 256.0),
                   reads=("QR",), writes=("MB",))
            b.emit("act", lambda e: e.activation(RB[:, :], QR[:, :], AF.Exp,
                                                 scale=-1.0 / 256.0),
                   reads=("QR",), writes=("RB",))
            tabs = [(0, VS, MB, QSC[l][:, 0:1]),
                    (512, SQB, MB, QSC[l][:, 1:2]),
                    (1024, VS, RB, KSC[l][:, 0:1]),
                    (1536, SQB, RB, KSC[l][:, 1:2])]
            for (off, t1, t2, sc) in tabs:
                b.emit("dve", lambda e, off=off, t1=t1, t2=t2: e.tensor_mul(
                    RT[:, off:off + 512], t1[:, :], t2[:, :]),
                    reads=("VS", "SQB", "MB", "RB"), writes=("RT",))
                b.emit("dve", lambda e, off=off, sc=sc: e.tensor_scalar_mul(
                    RT[:, off:off + 512], RT[:, off:off + 512], sc),
                    reads=("BI",), writes=("RT",))
            for i in range(4):
                b.dma(CTW, rott[l][i * 128:(i + 1) * 128, t0:t0 + 512],
                      RT[:, i * 512:(i + 1) * 512],
                      reads=("RT",), writes=(f"d:rott{l}",))

    b.barrier()


    def emit_ln_stats_and_apply(src_tile, src_cols, dst_writes, dst_slices,
                                affine=None):
        for kk in range(8):
            sq = SQA if kk % 2 == 0 else SQB
            sqz = "SQA" if kk % 2 == 0 else "SQB"
            src = src_cols(kk)
            b.emit("act", lambda e, sq=sq, src=src: e.activation(sq[:, :], src, AF.Square),
                   reads=(src_tile,), writes=(sqz,))
            b.emit("pe", _mmg([(PS[6][0:1, 0:512], ONES[:, 0:1], src,
                                kk == 0, kk == 7, None)]),
                   reads=(src_tile, "ONES"), writes=("ps6",))
            b.emit("pe", _mmg([(PS[7][0:1, 0:512], ONES[:, 0:1], sq[:, :],
                                kk == 0, kk == 7, None)]),
                   reads=(sqz, "ONES"), writes=("ps7",))
        b.emit("dve", lambda e: e.tensor_scalar_mul(MB[0:1, :], PS[6][0:1, 0:512], 1.0 / 1024.0),
               reads=("ps6",), writes=("MB",))
        b.emit("dve", lambda e: e.tensor_scalar_mul(RB[0:1, :], PS[7][0:1, 0:512], 1.0 / 1024.0),
               reads=("ps7",), writes=("RB",))
        b.emit("act", lambda e: e.activation(RSS[0:1, :], MB[0:1, :], AF.Square),
               reads=("MB",), writes=("RSS",))
        b.emit("dve", lambda e: e.tensor_sub(RB[0:1, :], RB[0:1, :], RSS[0:1, :]),
               reads=("RSS",), writes=("RB",))
        b.emit("act", lambda e: e.activation(RSS[0:1, :], RB[0:1, :], AF.Ln,
                                             bias=CEPS[0:1, 0:1]),
               reads=("RB", "CEPS"), writes=("RSS",))
        b.emit("act", lambda e: e.activation(RB[0:1, :], RSS[0:1, :], AF.Exp, scale=-0.5),
               reads=("RSS",), writes=("RB",))
        b.emit("pe", _mmg([(PS[6][:, 0:512], BONE[0:1, 0:128], MB[0:1, :],
                            True, True, None)]),
               reads=("MB", "BONE"), writes=("ps6",))
        b.emit("pe", _mmg([(PS[7][:, 0:512], BONE[0:1, 0:128], RB[0:1, :],
                            True, True, None)]),
               reads=("RB", "BONE"), writes=("ps7",))
        for kk in range(8):
            src = src_cols(kk)
            dst = dst_slices(kk)
            b.emit("dve", lambda e, d=dst, s=src: e.tensor_sub(d, s, PS[6][:, 0:512]),
                   reads=(src_tile, "ps6"), writes=dst_writes)
            b.emit("dve", lambda e, d=dst: e.tensor_mul(d, d, PS[7][:, 0:512]),
                   reads=("ps7",), writes=dst_writes)
            if affine is not None:
                g_ap, bt_ap = affine
                b.emit("dve", lambda e, d=dst, g=g_ap(kk), bb=bt_ap(kk): e.tensor_scalar(
                    d, d, g, bb, ALU.mult, ALU.add),
                    reads=("BI",), writes=dst_writes)

    def wslot_load(widx, dram_ap, j, nblk):
        s = widx % 2
        b.dma(CW[s], WS[s][:, 0:nblk * 128], dram_ap[j:j + 1, :],
              writes=(f"WS{s}",))
        return s

    for c in range(NC):
        ccol = c * 512
        for kk in range(8):
            b.dma(CX, X[:, kk * 512:(kk + 1) * 512],
                  xT[kk * 128:(kk + 1) * 128, ccol:ccol + 512], writes=("X",))
        emit_ln_stats_and_apply(
            "X", lambda kk: X[:, kk * 512:(kk + 1) * 512], ("XN",),
            dst_slices=lambda kk: XN[:, kk * 513 + 1:kk * 513 + 513])

        s0 = (2 * c) % 3
        s1 = (2 * c + 1) % 3
        wslot_load(0, wqkvA, 0, 8)
        for m in range(24):
            s = m % 2
            if m + 1 < 24:
                wslot_load(m + 1, wqkvA, m + 1, 8)
            parts = [(PS[s][:, 0:512], WS[s][:, kk * 128:(kk + 1) * 128],
                      XN[:, kk * 513 + 1:kk * 513 + 513], kk == 0, kk == 7, None)
                     for kk in range(8)]
            b.emit("pe", _mmg(parts), reads=(f"WS{s}", "XN"), writes=(psz[s],))
            if m < 8:
                b.emit("act", lambda e, m=m, s=s: e.copy(Qa(m * 512, (m + 1) * 512),
                                                         PS[s][:, 0:512]),
                       reads=(psz[s],), writes=("Q",))
            elif m < 16:
                j = m - 8
                d0 = KR[:, j * 768 + s0 * 256:j * 768 + s0 * 256 + 256]
                d1 = KR[:, j * 768 + s1 * 256:j * 768 + s1 * 256 + 256]
                b.emit("act", lambda e, d=d0, s=s: e.copy(d, PS[s][:, 0:256]),
                       reads=(psz[s],), writes=(f"KR{s0}",))
                b.emit("act", lambda e, d=d1, s=s: e.copy(d, PS[s][:, 256:512]),
                       reads=(psz[s],), writes=(f"KR{s1}",))
            else:
                j = m - 16
                b.emit("act", lambda e, s=s: e.copy(VS[:, :], PS[s][:, 0:512]),
                       reads=(psz[s],), writes=("VS",))
                for u in range(4):
                    b.emit("pe", lambda e, u=u: e.transpose(
                        PS[2][:, u * 128:(u + 1) * 128],
                        VS[:, u * 128:(u + 1) * 128], IDENT[:, :]),
                        reads=("VS", "IDENT"), writes=("ps2",))
                for half in range(2):
                    sl = s0 if half == 0 else s1
                    src = PS[2][:, half * 256:half * 256 + 256]
                    dst = bass.AP(VT, sl * 2048 + j * 128,
                                  [[6144, 128], [1024, 2], [1, 128]])
                    b.emit("act", lambda e, d=dst, sp=src: e.copy(d, sp),
                           reads=("ps2",), writes=(f"VT{sl}",))

        aunits = [(b_loc, hp, qs) for b_loc in range(2)
                  for hp in range(8) for qs in range(2)]

        def a1_env(u, ui):
            b_loc, hp, qs = u
            blk = 2 * c + b_loc
            return dict(b_loc=b_loc, hp=hp, qs=qs,
                        spv=(blk - 1) % 3, scv=blk % 3,
                        first=(blk == 0),
                        o_off=256 if blk == 0 else 0,
                        qcols=hp * 512 + b_loc * 256 + qs * 128,
                        cidx=((b_loc * 8 + hp) * 2 + qs) * 2,
                        nb=4 * (ui % 2),
                        sb0=3 if ui % 2 == 0 else 0)

        def a1_part1(u, ui):
            v = a1_env(u, ui)
            hp, qs, spv, scv, first = v["hp"], v["qs"], v["spv"], v["scv"], v["first"]
            o_off, qcols, nb = v["o_off"], v["qcols"], v["nb"]
            for h in range(2):
                sbk = v["sb0"] + h
                parts = []
                if not first:
                    parts.append((PS[sbk][:, 0:256],
                                  Qa(qcols, qcols + 128, 64 * h, 64 * h + 64),
                                  KR[64 * h:64 * h + 64,
                                     hp * 768 + spv * 256:hp * 768 + spv * 256 + 256],
                                  True, False, None))
                parts.append((PS[sbk][:, 256:512],
                              Qa(qcols, qcols + 128, 64 * h, 64 * h + 64),
                              KR[64 * h:64 * h + 64,
                                 hp * 768 + scv * 256:hp * 768 + scv * 256 + 256],
                              first, True, None))
                rd = ("Q", f"KR{scv}") if first else ("Q", f"KR{spv}", f"KR{scv}")
                b.emit("pe", _mmg(parts), reads=rd, writes=(psz[sbk],))
            for h in range(2):
                sbk = v["sb0"] + h
                b.emit("dve", lambda e, sbk=sbk, o=o_off, qs=qs: e.tensor_add(
                    PS[sbk][:, o:512], PS[sbk][:, o:512],
                    MA[:, qs * 512 + o:qs * 512 + 512]),
                    reads=("MA",), writes=(psz[sbk],))
                b.emit("dve", lambda e, sbk=sbk, o=o_off, h=h, nb=nb: e.reduce_max(
                    NMT[:, nb + h:nb + h + 1], PS[sbk][:, o:512], axis=AX.X,
                    negate=True),
                    reads=(psz[sbk],), writes=("NMT",))
            b.emit("dve", lambda e, nb=nb: e.tensor_scalar_mul(
                NMT[:, nb + 2:nb + 4], NMT[:, nb:nb + 2], 0.125),
                reads=(), writes=("NMT",))

        def a1_part2(u, ui):
            v = a1_env(u, ui)
            hp, qs, spv, scv, first = v["hp"], v["qs"], v["spv"], v["scv"], v["first"]
            o_off, qcols, cidx, nb = v["o_off"], v["qcols"], v["cidx"], v["nb"]
            for h in range(2):
                sbk = v["sb0"] + h
                b.emit("act", lambda e, sbk=sbk, o=o_off, h=h, ci=cidx, nb=nb: e.activation(
                    P[:, h * 512 + o:h * 512 + 512], PS[sbk][:, o:512], AF.Exp,
                    bias=NMT[:, nb + 2 + h:nb + 3 + h], scale=0.125,
                    accum_out=SM[:, ci + h:ci + h + 1]),
                    reads=(psz[sbk], "NMT"), writes=("P", "SM"))
            b.emit("pe", lambda e, ci=cidx: [
                e.transpose(PS[6][0:1, 0:128], SM[:, ci:ci + 1], IDENT[:, :]),
                e.transpose(PS[6][0:1, 128:256], SM[:, ci + 1:ci + 2], IDENT[:, :])],
                reads=("SM", "IDENT"), writes=("ps6",))
            b.emit("dve", lambda e: e.reciprocal(SMD[0:1, :], PS[6][0:1, 0:256]),
                   reads=("ps6",), writes=("SMD",))
            jl = [2, 3] if first else [0, 1, 2, 3]
            for h in range(2):
                ptb = 5 if h == 0 else 7
                for jj, j in enumerate(jl):
                    b.emit("pe", lambda e, h=h, j=j, jj=jj, ptb=ptb: e.transpose(
                        PS[ptb][:, jj * 128:(jj + 1) * 128],
                        P[:, h * 512 + j * 128:h * 512 + (j + 1) * 128],
                        IDENT[:, :]),
                        reads=("P", "IDENT"), writes=(psz[ptb],))
            b.emit("pe", _mmg([(PS[6][0:64, 0:128], BONE[0:1, 0:64],
                                SMD[0:1, 0:128], True, True, (0, 0)),
                               (PS[6][64:128, 0:128], BONE[0:1, 0:64],
                                SMD[0:1, 128:256], True, True, (0, 64))]),
                   reads=("SMD", "BONE"), writes=("ps6",))
            b.emit("act", lambda e: e.copy(RBC[:, :], PS[6][:, 0:128]),
                   reads=("ps6",), writes=("RBC",))
            for h in range(2):
                nj = len(jl)
                ptb = 5 if h == 0 else 7
                b.emit("act", lambda e, h=h, nj=nj, ptb=ptb: e.copy(
                    PTS[:, h * 512:h * 512 + nj * 128],
                    PS[ptb][:, 0:nj * 128]),
                    reads=(psz[ptb],), writes=("PTS",))
                parts = []
                vread = set()
                for jj, j in enumerate(jl):
                    sl = spv if j < 2 else scv
                    ts = j % 2
                    vread.add(f"VT{sl}")
                    base = sl * 2048 + ts * 1024 + (2 * hp + h) * 64
                    parts.append((PS[2][64 * h:64 * h + 64, 0:128],
                                  VT[:, base:base + 64],
                                  PTS[:, h * 512 + jj * 128:h * 512 + (jj + 1) * 128],
                                  jj == 0, jj == nj - 1, (0, 64 * h)))
                b.emit("pe", _mmg(parts), reads=tuple(vread) + ("PTS",),
                       writes=("ps2",))
            b.emit("dve", lambda e, qc=qcols: e.tensor_mul(
                O[:, qc:qc + 128], PS[2][:, 0:128], RBC[:, :]),
                reads=("ps2", "RBC"), writes=("O",))

        a1_part1(aunits[0], 0)
        for ui in range(1, len(aunits)):
            a1_part1(aunits[ui], ui)
            a1_part2(aunits[ui - 1], ui - 1)
        a1_part2(aunits[-1], len(aunits) - 1)

        wslot_load(0, woutA, 0, 8)
        for fo in range(8):
            s = fo % 2
            if fo + 1 < 8:
                wslot_load(fo + 1, woutA, fo + 1, 8)
            parts = [(PS[s][:, 0:512], WS[s][:, kk * 128:(kk + 1) * 128],
                      O[:, kk * 512:(kk + 1) * 512], kk == 0, kk == 7, None)
                     for kk in range(8)]
            b.emit("pe", _mmg(parts), reads=(f"WS{s}", "O"), writes=(psz[s],))
            b.emit("dve", lambda e, fo=fo, s=s: e.tensor_add(
                ACC[:, fo * 512:(fo + 1) * 512], PS[s][:, 0:512],
                X[:, fo * 512:(fo + 1) * 512]),
                reads=(psz[s], "X"), writes=(f"ACC{fo}",))
            b.dma(CO, acc_s[fo * 128:(fo + 1) * 128, ccol:ccol + 512],
                  ACC[:, fo * 512:(fo + 1) * 512],
                  reads=(f"ACC{fo}",), writes=("d:acc",))

    b.barrier()

    def sweep_ffn(l, src, dst):
        b.emit("dve", lambda e: e.memset(GXN[:, :], 0.0), writes=("GXN",))
        b.emit("dve", lambda e: e.memset(GX1[:, :], 0.0), writes=("GX1",))
        for c in range(NC):
            ccol = c * 512
            spv = (c - 1) % 2
            scv = c % 2
            first = (c == 0)
            if src == "x":
                for kk in range(8):
                    b.dma(CX, X[:, kk * 512:(kk + 1) * 512],
                          xT[kk * 128:(kk + 1) * 128, ccol:ccol + 512], writes=("X",))
                for kk in range(8):
                    b.emit("dve", lambda e, kk=kk: e.tensor_copy(
                        XN[:, kk * 513:kk * 513 + 1], GXN[:, kk:kk + 1]),
                        reads=("GXN",), writes=("XN",))
                emit_ln_stats_and_apply(
                    "X", lambda kk: X[:, kk * 512:(kk + 1) * 512], ("XN",),
                    dst_slices=lambda kk: XN[:, kk * 513 + 1:kk * 513 + 513])
                for kk in range(8):
                    b.emit("dve", lambda e, kk=kk: e.tensor_copy(
                        GXN[:, kk:kk + 1], XN[:, kk * 513 + 512:kk * 513 + 513]),
                        reads=("XN",), writes=("GXN",))
            else:
                for kk in range(8):
                    b.dma(CX, X[:, kk * 512:(kk + 1) * 512],
                          acc_s[kk * 128:(kk + 1) * 128, ccol:ccol + 512],
                          reads=("d:acc",), writes=("X",))
                if first:
                    for kk in range(8):
                        b.emit("dve", lambda e, kk=kk: e.memset(
                            XN[:, kk * 513:kk * 513 + 1], 0.0), writes=("XN",))
                        b.dma(CX, XN[:, kk * 513 + 1:kk * 513 + 513],
                              y1_s[kk * 128:(kk + 1) * 128, 0:512],
                              reads=("d:y1",), writes=("XN",))
                else:
                    for kk in range(8):
                        b.dma(CX, XN[:, kk * 513:kk * 513 + 513],
                              y1_s[kk * 128:(kk + 1) * 128, ccol - 1:ccol + 512],
                              reads=("d:y1",), writes=("XN",))
            for i in range(4):
                b.dma(CRT, RT[:, i * 512:(i + 1) * 512],
                      rott[l][i * 128:(i + 1) * 128, ccol:ccol + 512],
                      reads=(f"d:rott{l}",), writes=("RT",))

            def hshift(kk, off=0, w=512):
                base = kk * 513 + (1 if kk < 4 else 0) + off
                return XN[:, base:base + w]

            wslot_load(0, wqkvF[l], 0, 8)
            for m in range(12):
                s = m % 2
                if m + 1 < 12:
                    wslot_load(m + 1, wqkvF[l], m + 1, 8)
                parts = [(PS[s][:, 0:512], WS[s][:, kk * 128:(kk + 1) * 128],
                          hshift(kk), kk == 0, kk == 7, None) for kk in range(8)]
                b.emit("pe", _mmg(parts), reads=(f"WS{s}", "XN"), writes=(psz[s],))
                if m < 8:
                    b.emit("act", lambda e, s=s: e.activation(SQA[:, :], PS[s][:, 0:512],
                                                              AF.Square),
                           reads=(psz[s],), writes=("SQA",))
                    b.emit("pe", _mmg([(PS[5][0:1, 0:512], ONES2[:, 0:1], SQA[:, :],
                                        True, True, None),
                                       (PS[6][0:1, 0:512], ONES2[:, 1:2], SQA[:, :],
                                        True, True, None)]),
                           reads=("SQA", "ONES2"), writes=("ps5", "ps6"))
                    b.emit("dve", lambda e: e.tensor_scalar_max(RSS[0:1, :],
                                                                PS[5][0:1, 0:512], 1e-24),
                           reads=("ps5",), writes=("RSS",))
                    b.emit("dve", lambda e: e.tensor_scalar_max(P[0:1, 0:512],
                                                                PS[6][0:1, 0:512], 1e-24),
                           reads=("ps6",), writes=("P",))
                    b.emit("act", lambda e: e.activation(RSS[0:1, :], RSS[0:1, :], AF.Ln),
                           reads=(), writes=("RSS",))
                    b.emit("act", lambda e: e.activation(RSS[0:1, :], RSS[0:1, :], AF.Exp,
                                                         scale=-0.5),
                           reads=(), writes=("RSS",))
                    b.emit("act", lambda e: e.activation(P[0:1, 0:512], P[0:1, 0:512], AF.Ln),
                           reads=(), writes=("P",))
                    b.emit("act", lambda e: e.activation(P[0:1, 0:512], P[0:1, 0:512], AF.Exp,
                                                         scale=-0.5),
                           reads=(), writes=("P",))
                    b.emit("pe", _mmg([(PS[6][0:64, 0:512], BONE[0:1, 0:64],
                                        RSS[0:1, :], True, True, (0, 0)),
                                       (PS[6][64:128, 0:512], BONE[0:1, 0:64],
                                        P[0:1, 0:512], True, True, (0, 64))]),
                           reads=("RSS", "P", "BONE"), writes=("ps6",))
                    b.emit("act", lambda e: e.copy(MB[:, :], PS[6][:, 0:512]),
                           reads=("ps6",), writes=("MB",))
                    b.emit("dve", lambda e, s=s: e.tensor_mul(QR[:, :], PS[s][:, 0:512],
                                                              MB[:, :]),
                           reads=(psz[s], "MB"), writes=("QR",))
                    tb = 0 if m < 4 else 1024
                    if m < 4:
                        dzone = "Q"
                        dcol0, dcol1 = m * 512, (m + 1) * 512

                        def dsl(p0, p1, a=dcol0, bcol=dcol1):
                            return BIGQ[p0:p1, a:bcol]
                    else:
                        j = m - 4
                        dzone = f"KR{scv}"
                        dcol0 = j * 1024 + scv * 512
                        dcol1 = dcol0 + 512

                        def dsl(p0, p1, a=dcol0, bcol=dcol1):
                            return KR[p0:p1, a:bcol]
                    b.emit("pe", _mmg([(PS[5][:, 0:512], RROT[:, :], QR[:, :],
                                        True, True, None)]),
                           reads=("RROT", "QR"), writes=("ps5",))
                    b.emit("dve", lambda e, tb=tb, d=dsl(0, 128): e.tensor_mul(
                        d, QR[:, :], RT[:, tb:tb + 512]),
                        reads=("QR", "RT"), writes=(dzone,))
                    b.emit("dve", lambda e, tb=tb: e.tensor_mul(
                        VS[:, :], PS[5][:, 0:512], RT[:, tb + 512:tb + 1024]),
                        reads=("ps5", "RT"), writes=("VS",))
                    b.emit("dve", lambda e, d=dsl(0, 128): e.tensor_add(d, d, VS[:, :]),
                           reads=("VS",), writes=(dzone,))
                else:
                    j = m - 8
                    b.emit("act", lambda e, s=s: e.copy(VS[:, :], PS[s][:, 0:512]),
                           reads=(psz[s],), writes=("VS",))
                    for u in range(4):
                        b.emit("pe", lambda e, u=u: e.transpose(
                            PS[2][:, u * 128:(u + 1) * 128],
                            VS[:, u * 128:(u + 1) * 128], IDENT[:, :]),
                            reads=("VS", "IDENT"), writes=("ps2",))
                    dstv = bass.AP(VT, scv * 2048 + j * 128,
                                   [[6144, 128], [512, 4], [1, 128]])
                    b.emit("act", lambda e, d=dstv: e.copy(d, PS[2][:, 0:512]),
                           reads=("ps2",), writes=(f"VT{scv}",))

            funits = [(hp, qs, h) for hp in range(4) for qs in range(4)
                      for h in range(2)]

            def f_env(u, ui):
                hp, qs, h = u
                return dict(hp=hp, qs=qs, h=h,
                            qcols=hp * 512 + qs * 128,
                            scidx=(hp * 4 + qs) * 2 + h,
                            nbh=4 * (ui % 2),
                            bkA=2 + 2 * (ui % 2), bkB=3 + 2 * (ui % 2))

            def f_part1(u, ui):
                v = f_env(u, ui)
                hp, qs, h = v["hp"], v["qs"], v["h"]
                nbh, bkA, bkB = v["nbh"], v["bkA"], v["bkB"]
                if not first:
                    b.emit("pe", _mmg([(PS[bkA][:, 0:512],
                                        Qa(v["qcols"], v["qcols"] + 128, 64 * h, 64 * h + 64),
                                        KR[64 * h:64 * h + 64,
                                           hp * 1024 + spv * 512:hp * 1024 + spv * 512 + 512],
                                        True, True, None)]),
                           reads=("Q", f"KR{spv}"), writes=(psz[bkA],))
                b.emit("pe", _mmg([(PS[bkB][:, 0:512],
                                    Qa(v["qcols"], v["qcols"] + 128, 64 * h, 64 * h + 64),
                                    KR[64 * h:64 * h + 64,
                                       hp * 1024 + scv * 512:hp * 1024 + scv * 512 + 512],
                                    True, True, None)]),
                       reads=("Q", f"KR{scv}"), writes=(psz[bkB],))
                if not first:
                    b.emit("dve", lambda e, bk=bkA, qs=qs: e.tensor_add(
                        PS[bk][:, 0:512], PS[bk][:, 0:512],
                        MF[:, qs * 1024:qs * 1024 + 512]),
                        reads=("MF",), writes=(psz[bkA],))
                b.emit("dve", lambda e, bk=bkB, qs=qs: e.tensor_add(
                    PS[bk][:, 0:512], PS[bk][:, 0:512],
                    MF[:, qs * 1024 + 512:qs * 1024 + 1024]),
                    reads=("MF",), writes=(psz[bkB],))
                if not first:
                    b.emit("dve", lambda e, bk=bkA, nbh=nbh: e.reduce_max(
                        NMT[:, nbh:nbh + 1], PS[bk][:, 0:512], axis=AX.X, negate=True),
                        reads=(psz[bkA],), writes=("NMT",))
                b.emit("dve", lambda e, bk=bkB, nbh=nbh: e.reduce_max(
                    NMT[:, nbh + 1:nbh + 2], PS[bk][:, 0:512], axis=AX.X, negate=True),
                    reads=(psz[bkB],), writes=("NMT",))
                if not first:
                    b.emit("dve", lambda e, nbh=nbh: e.tensor_tensor(
                        NMT[:, nbh + 2:nbh + 3], NMT[:, nbh:nbh + 1],
                        NMT[:, nbh + 1:nbh + 2], ALU.min),
                        reads=(), writes=("NMT",))
                    b.emit("dve", lambda e, nbh=nbh: e.tensor_scalar_mul(
                        NMT[:, nbh + 3:nbh + 4], NMT[:, nbh + 2:nbh + 3], 8.0),
                        reads=(), writes=("NMT",))
                else:
                    b.emit("dve", lambda e, nbh=nbh: e.tensor_scalar_mul(
                        NMT[:, nbh + 3:nbh + 4], NMT[:, nbh + 1:nbh + 2], 8.0),
                        reads=(), writes=("NMT",))

            def f_part2(u, ui):
                v = f_env(u, ui)
                hp, qs, h = v["hp"], v["qs"], v["h"]
                nbh, bkA, bkB = v["nbh"], v["bkA"], v["bkB"]
                scidx = v["scidx"]
                qcols = v["qcols"]
                if not first:
                    b.emit("act", lambda e, bk=bkA, si=scidx, nbh=nbh: e.activation(
                        P[:, 0:512], PS[bk][:, 0:512], AF.Exp,
                        bias=NMT[:, nbh + 3:nbh + 4], scale=8.0,
                        accum_out=SM[:, 2 * si:2 * si + 1]),
                        reads=(psz[bkA], "NMT"), writes=("P", "SM"))
                    b.emit("act", lambda e, bk=bkB, si=scidx, nbh=nbh: e.activation(
                        P[:, 512:1024], PS[bk][:, 0:512], AF.Exp,
                        bias=NMT[:, nbh + 3:nbh + 4], scale=8.0,
                        accum_out=SM[:, 2 * si + 1:2 * si + 2]),
                        reads=(psz[bkB], "NMT"), writes=("P", "SM"))
                    b.emit("dve", lambda e, si=scidx: e.tensor_add(
                        SM[:, 64 + si:65 + si], SM[:, 2 * si:2 * si + 1],
                        SM[:, 2 * si + 1:2 * si + 2]),
                        reads=(), writes=("SM",))
                else:
                    b.emit("act", lambda e, bk=bkB, si=scidx, nbh=nbh: e.activation(
                        P[:, 512:1024], PS[bk][:, 0:512], AF.Exp,
                        bias=NMT[:, nbh + 3:nbh + 4], scale=8.0,
                        accum_out=SM[:, 64 + si:65 + si]),
                        reads=(psz[bkB], "NMT"), writes=("P", "SM"))
                last_pair = (h == 1)
                if last_pair:
                    si0 = (hp * 4 + qs) * 2
                    b.emit("pe", lambda e, si=si0: [
                        e.transpose(PS[1][0:1, 0:128], SM[:, 64 + si:65 + si],
                                    IDENT[:, :]),
                        e.transpose(PS[1][0:1, 128:256], SM[:, 65 + si:66 + si],
                                    IDENT[:, :])],
                        reads=("SM", "IDENT"), writes=("ps1",))
                    b.emit("dve", lambda e: e.reciprocal(SMD[0:1, :], PS[1][0:1, 0:256]),
                           reads=("ps1",), writes=("SMD",))
                jl = [4, 5, 6, 7] if first else list(range(8))
                for j in jl:
                    pb = 6 + j // 4
                    b.emit("pe", lambda e, j=j, pb=pb: e.transpose(
                        PS[pb][:, (j % 4) * 128:(j % 4) * 128 + 128],
                        P[:, j * 128:(j + 1) * 128], IDENT[:, :]),
                        reads=("P", "IDENT"), writes=(psz[pb],))
                if last_pair:
                    b.emit("pe", _mmg([(PS[1][0:64, 0:128], BONE[0:1, 0:64],
                                        SMD[0:1, 0:128], True, True, (0, 0)),
                                       (PS[1][64:128, 0:128], BONE[0:1, 0:64],
                                        SMD[0:1, 128:256], True, True, (0, 64))]),
                           reads=("SMD", "BONE"), writes=("ps1",))
                    b.emit("act", lambda e: e.copy(RBC[:, :], PS[1][:, 0:128]),
                           reads=("ps1",), writes=("RBC",))
                if not first:
                    b.emit("act", lambda e: e.copy(PTS[:, 0:512], PS[6][:, 0:512]),
                           reads=("ps6",), writes=("PTS",))
                b.emit("act", lambda e: e.copy(PTS[:, 512:1024], PS[7][:, 0:512]),
                       reads=("ps7",), writes=("PTS",))
                parts = []
                vread = set()
                for j in jl:
                    sl = spv if j < 4 else scv
                    ts = j % 4
                    vread.add(f"VT{sl}")
                    base = sl * 2048 + ts * 512 + (2 * hp + h) * 64
                    parts.append((PS[0][64 * h:64 * h + 64, 0:128],
                                  VT[:, base:base + 64],
                                  PTS[:, j * 128:(j + 1) * 128],
                                  j == jl[0], j == jl[-1], (0, 64 * h)))
                b.emit("pe", _mmg(parts), reads=tuple(vread) + ("PTS",),
                       writes=("ps0",))
                if last_pair:
                    b.emit("dve", lambda e, qc=qcols: e.tensor_mul(
                        O[:, qc:qc + 128], PS[0][:, 0:128], RBC[:, :]),
                        reads=("ps0", "RBC"), writes=("O",))

            f_part1(funits[0], 0)
            for ui in range(1, len(funits)):
                f_part1(funits[ui], ui)
                f_part2(funits[ui - 1], ui - 1)
            f_part2(funits[-1], len(funits) - 1)

            wslot_load(0, woF[l], 0, 4)
            for fo in range(8):
                s = fo % 2
                if fo + 1 < 8:
                    wslot_load(fo + 1, woF[l], fo + 1, 4)
                parts = [(PS[s][:, 0:512], WS[s][:, kk * 128:(kk + 1) * 128],
                          O[:, kk * 512:(kk + 1) * 512], kk == 0, kk == 3, None)
                         for kk in range(4)]
                b.emit("pe", _mmg(parts), reads=(f"WS{s}", "O"), writes=(psz[s],))
                b.emit("dve", lambda e, fo=fo, s=s: e.tensor_add(
                    X1s(fo * 513 + 1, fo * 513 + 513), PS[s][:, 0:512],
                    XN[:, fo * 513 + 1:fo * 513 + 513]),
                    reads=(psz[s], "XN"), writes=("X1",))
            for kk in range(8):
                b.emit("dve", lambda e, kk=kk: e.tensor_copy(
                    X1s(kk * 513, kk * 513 + 1), GX1[:, kk:kk + 1]),
                    reads=("GX1",), writes=("X1",))
            for kk in range(8):
                b.emit("dve", lambda e, kk=kk: e.tensor_copy(
                    GX1[:, kk:kk + 1], X1s(kk * 513 + 512, kk * 513 + 513)),
                    reads=("X1",), writes=("GX1",))

            def h2slice(kk):
                base = kk * 513 + (1 if kk < 4 else 0)
                return X1s(base, base + 512)

            emit_ln_stats_and_apply(
                "X1", h2slice, ("XN",),
                dst_slices=lambda kk: XN[:, kk * 513 + 1:kk * 513 + 513],
                affine=(lambda kk: LNG[l][:, kk:kk + 1], lambda kk: LNB[l][:, kk:kk + 1]))

            for ts2 in range(2):
                tw = ts2 * 256
                wslot_load(ts2 * 22, wfus[l], 0, 24)
                for m in range(22):
                    s = (ts2 * 22 + m) % 2
                    if m + 1 < 22:
                        wslot_load(ts2 * 22 + m + 1, wfus[l], m + 1, 24)
                    sa = m % 2
                    parts = [(PS[sa][:, 0:256], WS[s][:, kk * 128:(kk + 1) * 128],
                              XN[:, kk * 513 + 1 + tw:kk * 513 + 1 + tw + 256],
                              kk == 0, False, None) for kk in range(8)]
                    parts += [(PS[sa][:, 256:512], WS[s][:, (8 + kk) * 128:(9 + kk) * 128],
                               XN[:, kk * 513 + 1 + tw:kk * 513 + 1 + tw + 256],
                               False, kk == 7, None) for kk in range(8)]
                    b.emit("pe", _mmg(parts), reads=(f"WS{s}", "XN"), writes=(psz[sa],))
                    b.emit("act", lambda e, sa=sa, m=m: e.activation(
                        MG[:, :], PS[sa][:, 256:512], AF.Gelu,
                        bias=B1G[l][:, m:m + 1]),
                        reads=(psz[sa], "BI"), writes=("MG",))
                    b.emit("dve", lambda e, sa=sa, m=m: e.scalar_tensor_tensor(
                        Mt[:, :], PS[sa][:, 0:256], B1A[l][:, m:m + 1], MG[:, :],
                        ALU.add, ALU.mult),
                        reads=(psz[sa], "MG", "BI"), writes=("M",))
                    parts = [(PS[2 + fo // 2][:, (fo % 2) * 256:(fo % 2) * 256 + 256],
                              WS[s][:, (16 + fo) * 128:(17 + fo) * 128],
                              Mt[:, :], (m == 0 and fo % 2 == 0),
                              (m == 21 and fo % 2 == 1), None)
                             for fo in range(8)]
                    b.emit("pe", _mmg(parts), reads=(f"WS{s}", "M"),
                           writes=("ps2", "ps3", "ps4", "ps5"))
                for fo in range(8):
                    w2v = PS[2 + fo // 2][:, (fo % 2) * 256:(fo % 2) * 256 + 256]
                    b.emit("dve", lambda e, fo=fo, w2v=w2v, tw=tw: e.scalar_tensor_tensor(
                        ACC[:, fo * 512 + tw:fo * 512 + tw + 256], w2v,
                        B2[l][:, fo:fo + 1],
                        X1s(fo * 513 + 1 + tw, fo * 513 + 1 + tw + 256),
                        ALU.add, ALU.add),
                        reads=("ps" + str(2 + fo // 2), "X1", "BI"),
                        writes=(f"ACC{fo}",))
                    if dst == "out":
                        b.emit("dve", lambda e, fo=fo, tw=tw: e.tensor_add(
                            ACC[:, fo * 512 + tw:fo * 512 + tw + 256],
                            ACC[:, fo * 512 + tw:fo * 512 + tw + 256],
                            X[:, fo * 512 + tw:fo * 512 + tw + 256]),
                            reads=("X",), writes=(f"ACC{fo}",))
            dstd = y1_s if dst == "y1" else outT
            dzone = "d:y1" if dst == "y1" else "d:out"
            for fo in range(8):
                b.dma(CO, dstd[fo * 128:(fo + 1) * 128, ccol:ccol + 512],
                      ACC[:, fo * 512:(fo + 1) * 512],
                      reads=(f"ACC{fo}",), writes=(dzone,))

    sweep_ffn(0, "x", "y1")
    b.barrier()
    sweep_ffn(1, "y1", "out")

    b.wait_all_chans("sp")
    b.replay()
    nc.finalize()
    return nc



def _tile_w(w, nk, nm):
    K, M = w.shape
    assert K == nk * 128 and M == nm * 128
    t = w.reshape(nk, 128, nm, 128).transpose(2, 1, 0, 3)
    return np.ascontiguousarray(t).reshape(nm, 128 * nk * 128)


def _pad_cols(w, M):
    out = np.zeros((w.shape[0], M), dtype=w.dtype)
    out[:, :w.shape[1]] = w
    return out


def _pad_rows(w, K):
    out = np.zeros((K, w.shape[1]), dtype=w.dtype)
    out[:w.shape[0]] = w
    return out


def stage_inputs(L0_attn_w_qkv, L0_attn_w_out, L0_ffn_wqkv, L0_ffn_q_scale,
                 L0_ffn_k_scale, L0_ffn_wo, L0_ffn_ln_g, L0_ffn_ln_b,
                 L0_ffn_w1, L0_ffn_b1, L0_ffn_w2, L0_ffn_b2):
    f = np.float32
    d = {}
    d["wqkvA"] = _tile_w(np.asarray(L0_attn_w_qkv, f), 8, 24)
    d["woutA"] = _tile_w(np.asarray(L0_attn_w_out, f), 8, 8)
    for l in range(2):
        d[f"wqkvF{l}"] = _tile_w(np.asarray(L0_ffn_wqkv[l], f), 8, 12)
        d[f"woF{l}"] = _tile_w(np.asarray(L0_ffn_wo[l], f), 4, 8)
        w1 = np.asarray(L0_ffn_w1[l], f)
        wa = _pad_cols(w1[:, :2730], 2816).reshape(8, 128, 22, 128).transpose(2, 1, 0, 3)
        wg = _pad_cols(w1[:, 2730:], 2816).reshape(8, 128, 22, 128).transpose(2, 1, 0, 3)
        w2 = _pad_rows(np.asarray(L0_ffn_w2[l], f), 2816).reshape(22, 128, 8, 128)
        d[f"wfus{l}"] = np.ascontiguousarray(
            np.concatenate([wa, wg, w2], axis=2)).reshape(22, 128 * 24 * 128)
        b1 = np.asarray(L0_ffn_b1[l], f)
        d[f"b1a{l}"] = np.ascontiguousarray(
            np.concatenate([b1[:2730], np.zeros(86, f)]).reshape(22, 128).T)
        d[f"b1g{l}"] = np.ascontiguousarray(
            np.concatenate([b1[2730:], np.zeros(86, f)]).reshape(22, 128).T)
        d[f"b2{l}"] = np.ascontiguousarray(np.asarray(L0_ffn_b2[l], f).reshape(8, 128).T)
        d[f"lng{l}"] = np.ascontiguousarray(np.asarray(L0_ffn_ln_g[l], f).reshape(8, 128).T)
        d[f"lnb{l}"] = np.ascontiguousarray(np.asarray(L0_ffn_ln_b[l], f).reshape(8, 128).T)
        qs = np.asarray(L0_ffn_q_scale[l], f)
        ks = np.asarray(L0_ffn_k_scale[l], f)
        d[f"qsc{l}"] = np.ascontiguousarray(
            np.stack([np.tile(qs, 2), np.tile(np.roll(qs, -32), 2)], axis=1))
        d[f"ksc{l}"] = np.ascontiguousarray(
            np.stack([np.tile(ks, 2), np.tile(np.roll(ks, -32), 2)], axis=1))
    return d


def kernel(L0_attn_x, L0_attn_w_qkv, L0_attn_w_out, L0_ffn_wqkv, L0_ffn_q_scale,
           L0_ffn_k_scale, L0_ffn_wo, L0_ffn_ln_g, L0_ffn_ln_b, L0_ffn_w1,
           L0_ffn_b1, L0_ffn_w2, L0_ffn_b2):
    x = np.asarray(L0_attn_x, np.float32)
    B, NT, D = x.shape
    nc = build_core_program(NT)
    wd = stage_inputs(L0_attn_w_qkv, L0_attn_w_out, L0_ffn_wqkv, L0_ffn_q_scale,
                      L0_ffn_k_scale, L0_ffn_wo, L0_ffn_ln_g, L0_ffn_ln_b,
                      L0_ffn_w1, L0_ffn_b1, L0_ffn_w2, L0_ffn_b2)
    in_maps = []
    for bb in range(B):
        m = {"xT": np.ascontiguousarray(x[bb].T)}
        m.update(wd)
        in_maps.append(m)
    res = run_bass_kernel_spmd(nc, in_maps, core_ids=list(range(B)))
    out = np.stack([np.asarray(r["outT"]).T for r in res.results], axis=0)
    return out.astype(np.float32)
